# version 09 (final)
# baseline (speedup 1.0000x reference)
import contextlib
import numpy as np
import concourse.bass as bass
import concourse.mybir as mybir
from concourse.bass_utils import run_bass_kernel_spmd

F32 = mybir.dt.float32
BF16 = mybir.dt.bfloat16
AF = mybir.ActivationFunctionType
OP = mybir.AluOpType
AX = mybir.AxisListType

P, D, T = 128, 1024, 4096
NT, W, NW, NCH = 32, 512, 8, 8
H, DH, FF, NE, I2 = 16, 64, 4096, 8, 2048
EPS = 1e-5

ENGINES = ("sync", "gpsimd", "vector", "scalar", "tensor")
OWN = {"vector": "sV", "scalar": "sS", "tensor": "sT", "gpsimd": "sG", "sync": None}


class Buf:
    __slots__ = ("w", "r")

    def __init__(self):
        self.w = None
        self.r = []


DMA_POOLS = {"sync": ["sX%d" % i for i in range(10)],
             "gpsimd": ["sW%d" % i for i in range(6)]}


class Rec:
    def __init__(self):
        self.items = {e: [] for e in ENGINES}
        self.cnt = {}
        self.guard = {}
        self.dma_rr = {e: 0 for e in DMA_POOLS}

    def v(self, sem):
        return self.cnt.get(sem, 0)

    def emit(self, eng, fn, reads=(), writes=(), extra=(), dma=None):
        waits = {}
        for b in reads:
            if b.w is not None:
                s, v = b.w
                waits[s] = max(waits.get(s, 0), v)
        for b in writes:
            for dep in ([b.w] if b.w else []) + b.r:
                s, v = dep
                waits[s] = max(waits.get(s, 0), v)
        for s, v in extra:
            waits[s] = max(waits.get(s, 0), v)
        own = OWN[eng]
        if dma is not None:
            pool = DMA_POOLS[eng]
            sem = pool[self.dma_rr[eng] % len(pool)]
            self.dma_rr[eng] += 1
            prev = self.cnt.get(sem, 0)
            if prev > 0:
                waits[sem] = max(waits.get(sem, 0), prev)
            n = 16
        else:
            sem, n = own, 1
        w2 = []
        for s, v in waits.items():
            if v <= 0 or s == own:
                continue
            k = (eng, s)
            if self.guard.get(k, 0) >= v:
                continue
            self.guard[k] = v
            w2.append((s, v))
        assert sem is not None
        self.items[eng].append((fn, w2, sem, n))
        self.cnt[sem] = self.cnt.get(sem, 0) + n
        val = self.cnt[sem]
        for b in writes:
            b.w = (sem, val)
            b.r = []
        for b in reads:
            b.r.append((sem, val))
        return sem, val


def build(n_instr_out=None, debug=False):
    nc = bass.Bass("TRN2", target_bir_lowering=False)

    h_x = nc.dram_tensor("x", [T, D], F32, kind="ExternalInput").ap()
    h_wq = nc.dram_tensor("wq", [D, D], F32, kind="ExternalInput").ap()
    h_wk = nc.dram_tensor("wk", [D, D], F32, kind="ExternalInput").ap()
    h_wv = nc.dram_tensor("wv", [D, D], F32, kind="ExternalInput").ap()
    h_wo = nc.dram_tensor("wo", [D, D], F32, kind="ExternalInput").ap()
    h_g1c = nc.dram_tensor("g1c", [P, NCH], F32, kind="ExternalInput").ap()
    h_b1c = nc.dram_tensor("b1c", [P, NCH], F32, kind="ExternalInput").ap()
    h_g2c = nc.dram_tensor("g2c", [P, NCH], F32, kind="ExternalInput").ap()
    h_b2c = nc.dram_tensor("b2c", [P, NCH], F32, kind="ExternalInput").ap()
    h_w1 = nc.dram_tensor("w1", [D, FF], F32, kind="ExternalInput").ap()
    h_w2 = nc.dram_tensor("w2", [FF, D], F32, kind="ExternalInput").ap()
    h_gatet = nc.dram_tensor("gatet", [D, NE], F32, kind="ExternalInput").ap()
    h_mw1t = nc.dram_tensor("mw1t", [NE, D, D], F32, kind="ExternalInput").ap()
    h_mw2t = nc.dram_tensor("mw2t", [NE, D, D], F32, kind="ExternalInput").ap()
    h_mw3t = nc.dram_tensor("mw3t", [NE, D, D], F32, kind="ExternalInput").ap()
    h_be1 = nc.dram_tensor("be1", [NE, P, D], F32, kind="ExternalInput").ap()
    h_be3 = nc.dram_tensor("be3", [NE, P, D], F32, kind="ExternalInput").ap()
    h_be2r = nc.dram_tensor("be2r", [NE, 1, D], F32, kind="ExternalInput").ap()
    h_sw1t = nc.dram_tensor("sw1t", [D, I2], F32, kind="ExternalInput").ap()
    h_sw2t = nc.dram_tensor("sw2t", [D, I2], F32, kind="ExternalInput").ap()
    h_sw3t = nc.dram_tensor("sw3t", [I2, D], F32, kind="ExternalInput").ap()
    h_ident = nc.dram_tensor("ident", [P, P], F32, kind="ExternalInput").ap()
    h_ones = nc.dram_tensor("onesrow", [1, W], F32, kind="ExternalInput").ap()
    h_out = nc.dram_tensor("out", [T, D], F32, kind="ExternalOutput").ap()
    KI = "ExternalOutput"
    if debug:
        h_dbg_wbd = nc.dram_tensor("dbg_wbd", [P, 8192], BF16, kind="ExternalOutput").ap()
        h_dbg_wz = nc.dram_tensor("dbg_wz", [P, 128], BF16, kind="ExternalOutput").ap()
        h_dbg_small = nc.dram_tensor("dbg_small", [P, 2176], F32, kind="ExternalOutput").ap()
    h_xatt = nc.dram_tensor("sp_xatt", [T, D], BF16, kind=KI).ap()
    h_xnew = nc.dram_tensor("sp_xnew", [T, D], BF16, kind=KI).ap()
    h_phiq = nc.dram_tensor("sp_phiq", [NT, P, D], BF16, kind=KI).ap()
    h_h2t = nc.dram_tensor("sp_h2t", [NW, P, NCH * W], BF16, kind=KI).ap()
    h_u2t = nc.dram_tensor("sp_u2t", [NT, P, D], BF16, kind=KI).ap()

    es = contextlib.ExitStack()
    sbt = lambda name, shape, dt: es.enter_context(nc.sbuf_tensor(name, shape, dt))

    ubf = sbt("ubf", [P, NT * D], BF16)
    wts_a = sbt("wts_a", [P, 8192], BF16)
    wts_b = sbt("wts_b", [P, 8192], BF16)
    wts_c = sbt("wts_c", [P, 8192], BF16)
    f32a = sbt("f32a", [P, 8192], F32)
    small = sbt("small", [P, 2176], F32)
    wins = sbt("wins", [P, 4096], BF16)
    misc = sbt("misc", [P, 5184], BF16)
    h2w = sbt("h2w", [P, 4096], BF16)
    ftile = sbt("ftile", [P, 2 * D], BF16)
    pstage = sbt("pstage", [P, 2 * W], BF16)
    brow = sbt("brow", [1, 8704], BF16)
    bcol = sbt("bcol", [P, 2 * NCH], BF16)
    ident = sbt("identt", [P, P], BF16)
    betile = sbt("betile", [P, 2 * D], BF16)

    ps_a = es.enter_context(nc.psum_tensor("ps_a", [P, 1024], F32))
    ps_b = es.enter_context(nc.psum_tensor("ps_b", [P, 1024], F32))
    ps_c = es.enter_context(nc.psum_tensor("ps_c", [P, 1024], F32))
    ps_d = es.enter_context(nc.psum_tensor("ps_d", [P, 512], F32))
    ps_t = es.enter_context(nc.psum_tensor("ps_t", [P, 1024], BF16))

    sems = {}
    for s in ["sV", "sS", "sT", "sG"] + DMA_POOLS["sync"] + DMA_POOLS["gpsimd"]:
        sems[s] = es.enter_context(nc.semaphore(s))

    R = Rec()

    ST = {}
    off = 0
    for name in ("sx1", "sq1", "mu1", "r1", "rr2", "sx3", "sq3", "mu3", "r3",
                 "sx4", "sq4", "mu4", "r4", "sx5", "sq5", "mu5", "r5",
                 "sx6", "sq6", "mu6", "r6"):
        ST[name] = off; off += NT
    C_scr = off; off += 2 * NT
    C_g1 = off; off += NCH
    C_g2 = off; off += NCH
    C_half = off; off += 1
    C_gb = off; off += 1
    C_eps = off; off += 1
    C_kv = off; off += 520
    C_logit = off; off += 256
    C_wts = off; off += 256
    C_msk = off; off += 256
    C_rsc = off; off += NT
    C_den = off; off += 32
    C_spc = off; off += 1
    C_rsc2 = off; off += NT
    assert off <= 2176, off

    def stc(name, t):
        return small[:, ST[name] + t:ST[name] + t + 1]

    def strow(name):
        return small[:, ST[name]:ST[name] + NT]

    XA = [f32a[:, 0:1024], f32a[:, 1024:2048]]
    S0 = f32a[:, 2048:3072]
    S1 = f32a[:, 3072:4096]
    WARR = f32a[:, 0:2048]
    CARRY = f32a[:, 2048:4096]

    M0 = misc[:, 0:1024]
    M1 = misc[:, 1024:2048]
    V65 = misc[:, 2048:3088]
    GS_H = [misc[:, 0:2048], misc[:, 3136:5184]]
    WARR_H = [f32a[:, 0:2048], f32a[:, 4096:6144]]
    CARRY_H = [f32a[:, 2048:4096], f32a[:, 6144:8192]]

    B_wq, B_wk, B_wv, B_b2w1, B_ones, B_b2e = 0, 1024, 2048, 3072, 7168, 7680

    bufs = {}
    for t in range(NT):
        bufs["ubf%d" % t] = Buf()
    for n in ("wa", "wb", "wc", "xa0", "xa1", "s0", "s1", "wins", "m0", "m1",
              "v65", "kv", "stats", "rt", "br_q", "br_k", "br_v",
              "br_b2w1", "br_ones", "br_b2e", "bcol", "ident", "betile", "h2w",
              "ft0", "ft1", "pg0", "pg1", "w1r", "c1r", "gs1", "wins1",
              "pa", "pb", "pc", "pd", "pt",
              "xatt_h", "xnew_h", "phiq_h", "h2t_h", "u2t_h", "glob"):
        bufs[n] = Buf()
    B = lambda n: bufs[n]
    UB = lambda t: bufs["ubf%d" % t]

    FT = [ftile[:, 0:D], ftile[:, D:2 * D]]
    FTB = [B("ft0"), B("ft1")]
    PG = [pstage[:, 0:W], pstage[:, W:2 * W]]
    PGB = [B("pg0"), B("pg1")]

    def ttile(buf, t):
        return buf[:, t * D:(t + 1) * D]

    R.emit("gpsimd", lambda g: g.dma_start(out=ident[:], in_=h_ident[:]), writes=[B("ident")], dma=True)
    R.emit("gpsimd", lambda g: g.dma_start(out=bcol[:, 0:NCH], in_=h_b1c[:]), writes=[B("bcol")], dma=True)
    R.emit("gpsimd", lambda g: g.dma_start(out=bcol[:, NCH:2 * NCH], in_=h_b2c[:]), reads=[B("bcol")],
           writes=[B("bcol")], dma=True)
    R.emit("gpsimd", lambda g: g.dma_start(out=brow[0:1, B_ones:B_ones + W], in_=h_ones[:]),
           writes=[B("br_ones")], dma=True)
    for dst, src, bn in ((wts_a, h_wq, "wa"), (wts_b, h_wk, "wb"), (wts_c, h_wv, "wc")):
        R.emit("gpsimd", lambda g, d=dst, s=src: g.dma_start(out=d[:], in_=s.rearrange("(c p) n -> p c n", p=P)),
               writes=[B(bn)], dma=True)
    R.emit("sync", lambda s: s.dma_start(out=small[:, C_g1:C_g1 + NCH], in_=h_g1c[:]), writes=[B("glob")], dma=True)
    R.emit("sync", lambda s: s.dma_start(out=small[:, C_g2:C_g2 + NCH], in_=h_g2c[:]), reads=[B("glob")],
           writes=[B("glob")], dma=True)
    R.emit("vector", lambda v: v.memset(small[:, C_half:C_half + 1], 0.5), writes=[B("glob")])
    R.emit("vector", lambda v: v.memset(small[:, C_gb:C_gb + 1], -0.05), writes=[B("glob")])
    R.emit("vector", lambda v: v.memset(small[:, C_eps:C_eps + 1], EPS), writes=[B("glob")])
    R.emit("vector", lambda v: v.memset(small[:, C_kv:C_kv + 520], 0.0), writes=[B("kv")])
    R.emit("vector", lambda v: v.memset(V65, 1.0), writes=[B("v65")])

    for bn, wbuf in (("wa", wts_a), ("wb", wts_b), ("wc", wts_c)):
        for c in range(NCH):
            R.emit("vector", lambda v, wb=wbuf, cc=c: v.tensor_scalar(
                wb[:, cc * D:(cc + 1) * D], wb[:, cc * D:(cc + 1) * D],
                small[:, C_g1 + cc:C_g1 + cc + 1], None, OP.mult),
                reads=[B("glob")], writes=[B(bn)])

    for wbuf, bn, bofs, brn in ((wts_a, "wa", B_wq, "br_q"), (wts_b, "wb", B_wk, "br_k"),
                                (wts_c, "wc", B_wv, "br_v")):
        for j in range(2):
            def mmf(t, wb=wbuf, jj=j):
                ins = None
                for c in range(NCH):
                    ins = t.matmul(ps_d[0:1, 0:512], bcol[:, c:c + 1],
                                   wb[:, c * D + jj * 512:c * D + jj * 512 + 512],
                                   start=(c == 0), stop=(c == NCH - 1))
                return ins
            R.emit("tensor", mmf, reads=[B(bn), B("bcol")], writes=[B("pd")])
            R.emit("vector", lambda v, bo=bofs, jj=j: v.tensor_copy(
                brow[0:1, bo + jj * 512:bo + jj * 512 + 512], ps_d[0:1, 0:512]),
                reads=[B("pd")], writes=[B(brn)])

    for t in range(NT):
        xa, xab = (XA[0], B("xa0")) if t % 2 == 0 else (XA[1], B("xa1"))
        R.emit("sync", lambda s, tt=t, d=xa: s.dma_start(out=d, in_=h_x[tt * P:(tt + 1) * P, :]),
               writes=[xab], dma=True)
        R.emit("vector", lambda v, tt=t, d=xa: v.reduce_sum(stc("sx1", tt), d, axis=AX.X),
               reads=[xab], writes=[B("stats")])
        R.emit("scalar", lambda s, tt=t, d=xa: s.activation(S0, d, AF.Square, accum_out=stc("sq1", tt)),
               reads=[xab], writes=[B("s0"), B("stats")])

    def chain_ln(sx, sq, mu, r, rr2=False):
        scr0 = small[:, C_scr:C_scr + NT]
        scr1 = small[:, C_scr + NT:C_scr + 2 * NT]
        def fn(v):
            v.tensor_scalar(strow(mu), strow(sx), 1.0 / D, None, OP.mult)
            v.tensor_scalar(scr0, strow(sq), 1.0 / D, None, OP.mult)
            v.drain()
            v.tensor_tensor(scr1, strow(mu), strow(mu), OP.mult)
            v.drain()
            return v.tensor_tensor(scr0, scr0, scr1, OP.subtract)
        R.emit("vector", fn, reads=[B("stats")], writes=[B("stats")])
        R.emit("scalar", lambda s: s.activation(scr1, scr0, AF.Sqrt, bias=small[:, C_eps:C_eps + 1]),
               reads=[B("stats"), B("glob")], writes=[B("stats")])
        def recf(v):
            v.reciprocal(strow(r), scr1)
            return v.drain()
        R.emit("vector", recf, reads=[B("stats")], writes=[B("stats")])
        if rr2:
            def fn2(v):
                v.tensor_tensor(scr1, strow(r), strow(r), OP.mult)
                v.drain()
                return v.tensor_tensor(scr1, scr1, scr0, OP.mult)
            R.emit("vector", fn2, reads=[B("stats")], writes=[B("stats")])
            R.emit("scalar", lambda s: s.activation(scr1, scr1, AF.Sqrt, bias=small[:, C_eps:C_eps + 1]),
                   reads=[B("stats"), B("glob")], writes=[B("stats")])
            def fn3(v):
                v.reciprocal(scr0, scr1)
                v.drain()
                return v.tensor_tensor(strow("rr2"), scr0, strow(r), OP.mult)
            R.emit("vector", fn3, reads=[B("stats")], writes=[B("stats")])

    chain_ln("sx1", "sq1", "mu1", "r1", rr2=True)

    def transpose8(src_ap, src_bufs, dst_ap, dst_bufs):
        def tf(t_):
            ins = None
            for c in range(NCH):
                ins = t_.transpose(ps_t[:, c * P:(c + 1) * P], src_ap[:, c * P:(c + 1) * P], ident[:])
            return ins
        R.emit("tensor", tf, reads=list(src_bufs) + [B("ident")], writes=[B("pt")])
        R.emit("vector", lambda v: v.tensor_copy(dst_ap, ps_t[:].rearrange("p (c q) -> p c q", q=P)),
               reads=[B("pt")], writes=list(dst_bufs))

    for w in range(NW):
        w0 = w * W
        for i in range(4):
            t = 4 * w + i
            xa, xab = (XA[0], B("xa0")) if t % 2 == 0 else (XA[1], B("xa1"))
            R.emit("sync", lambda s, tt=t, d=xa: s.dma_start(out=d, in_=h_x[tt * P:(tt + 1) * P, :]),
                   writes=[xab], dma=True)
            R.emit("vector", lambda v, tt=t, d=xa: v.tensor_scalar(
                ttile(ubf, tt), d, stc("mu1", tt), stc("r1", tt), OP.subtract, OP.mult),
                reads=[xab, B("stats")], writes=[UB(t)])
            R.emit("vector", lambda v, tt=t, d=xa: v.tensor_scalar(
                M0, d, stc("mu1", tt), stc("rr2", tt), OP.subtract, OP.mult),
                reads=[xab, B("stats")], writes=[B("m0")])
            dst = wins[:].rearrange("p (c q) -> p c q", q=W)[:, :, i * P:(i + 1) * P]
            transpose8(M0, [B("m0")], dst, [B("wins"), B("wins1")])

        for co in range(NCH):
            pg, pgb = PG[co % 2], PGB[co % 2]
            def qmm(t_, cc=co):
                ins = None
                for c in range(NCH):
                    ins = t_.matmul(ps_d[:, 0:512], wts_a[:, c * D + cc * P:c * D + cc * P + P],
                                    wins[:, c * W:(c + 1) * W], start=(c == 0), stop=False)
                return t_.matmul(ps_d[:, 0:512], brow[0:1, B_wq + cc * P:B_wq + cc * P + P],
                                 brow[0:1, B_ones:B_ones + W], start=False, stop=True)
            R.emit("tensor", qmm, reads=[B("wins"), B("wins1"), B("wa"), B("br_q"), B("br_ones")], writes=[B("pd")])
            R.emit("vector", lambda v: v.tensor_scalar(S0[:, 0:W], ps_d[:, 0:512], 0.0, None, OP.min),
                   reads=[B("pd")], writes=[B("s0")])
            R.emit("scalar", lambda s: s.activation(S1[:, 0:W], S0[:, 0:W], AF.Exp),
                   reads=[B("s0")], writes=[B("s1")])
            R.emit("vector", lambda v, p_=pg: v.scalar_tensor_tensor(
                p_, ps_d[:, 0:512], 0.0, S1[:, 0:W], OP.max, OP.add),
                reads=[B("pd"), B("s1")], writes=[pgb])
            R.emit("sync", lambda s, cc=co, ww=w, p_=pg: s.dma_start(
                out=h_phiq[4 * ww:4 * ww + 4, :, cc * P:(cc + 1) * P].rearrange("i p q -> p i q"),
                in_=p_.rearrange("p (i q) -> p i q", q=P)),
                reads=[pgb], writes=[B("phiq_h")], dma=True)

        for i in range(4):
            t = 4 * w + i
            def kvw_mm(t_, ii, wbuf, psum, brofs):
                ins = None
                for j in range(2):
                    sl = slice(j * 512, (j + 1) * 512)
                    for c in range(NCH):
                        ins = t_.matmul(psum[:, sl], wins[:, c * W + ii * P:c * W + ii * P + P],
                                        wbuf[:, c * D:(c + 1) * D][:, sl], start=(c == 0), stop=False)
                    ins = t_.matmul(psum[:, sl], brow[0:1, B_ones:B_ones + P],
                                    brow[0:1, brofs:brofs + D][:, sl], start=False, stop=True)
                return ins
            R.emit("tensor", lambda t_, ii=i: kvw_mm(t_, ii, wts_b, ps_a, B_wk),
                   reads=[B("wins"), B("wins1"), B("wb"), B("br_k"), B("br_ones")], writes=[B("pa")])
            R.emit("vector", lambda v: v.tensor_scalar(S0, ps_a[:], 0.0, None, OP.min),
                   reads=[B("pa")], writes=[B("s0")])
            R.emit("scalar", lambda s: s.activation(S1, S0, AF.Exp), reads=[B("s0")], writes=[B("s1")])
            R.emit("vector", lambda v: v.scalar_tensor_tensor(M0, ps_a[:], 0.0, S1, OP.max, OP.add),
                   reads=[B("pa"), B("s1")], writes=[B("m0")])
            R.emit("tensor", lambda t_, ii=i: kvw_mm(t_, ii, wts_c, ps_b, B_wv),
                   reads=[B("wins"), B("wins1"), B("wc"), B("br_v"), B("br_ones")], writes=[B("pb")])
            def v65f(v):
                src = ps_b[:].rearrange("p (h e) -> p h e", e=DH)
                dst = V65.rearrange("p (h e) -> p h e", e=65)[:, :, 0:DH]
                return v.tensor_copy(dst, src)
            R.emit("vector", v65f, reads=[B("pb")], writes=[B("v65")])
            def kvmm(t_):
                ins = None
                for h in range(H):
                    half, j = h % 2, h // 2
                    base = 64 * half
                    ins = t_.matmul(ps_c[base:base + 64, 64 * j:64 * j + 64],
                                    M0[:, DH * h:DH * h + DH], V65[:, 65 * h:65 * h + DH],
                                    start=True, stop=True, tile_position=(0, base))
                    ins = t_.matmul(ps_c[base:base + 64, 512 + j:513 + j],
                                    M0[:, DH * h:DH * h + DH], V65[:, 64:65],
                                    start=True, stop=True, tile_position=(0, base))
                return ins
            R.emit("tensor", kvmm, reads=[B("m0"), B("v65")], writes=[B("pc")])
            R.emit("vector", lambda v: v.tensor_tensor(small[:, C_kv:C_kv + 520], small[:, C_kv:C_kv + 520],
                                                       ps_c[:, 0:520], OP.add),
                   reads=[B("pc"), B("kv")], writes=[B("kv")])

    R.emit("gpsimd", lambda g: g.dma_start(out=wts_c[:], in_=h_wo.rearrange("(c p) n -> p c n", p=P)),
           writes=[B("wc")], dma=True)
    R.emit("vector", lambda v: v.memset(wts_b[:], 0.0), writes=[B("wb")])
    def wbd(v):
        ins = None
        for c in range(NCH):
            base = c * D + 128 * c
            ins = v.tensor_copy(wts_b[0:64, base:base + DH], small[0:64, C_kv + 64 * c:C_kv + 64 * c + DH])
            ins = v.tensor_copy(wts_b[64:128, base + DH:base + 2 * DH],
                                small[64:128, C_kv + 64 * c:C_kv + 64 * c + DH])
        return ins
    R.emit("vector", wbd, reads=[B("kv")], writes=[B("wb")])
    R.emit("vector", lambda v: v.memset(wts_a[:, 0:128], 0.0), writes=[B("wa")])
    def wz(v):
        ins = None
        for c in range(NCH):
            ins = v.tensor_copy(wts_a[0:64, 16 * c + 2 * c:16 * c + 2 * c + 1],
                                small[0:64, C_kv + 512 + c:C_kv + 512 + c + 1])
            ins = v.tensor_copy(wts_a[64:128, 16 * c + 2 * c + 1:16 * c + 2 * c + 2],
                                small[64:128, C_kv + 512 + c:C_kv + 512 + c + 1])
        return ins
    R.emit("vector", wz, reads=[B("kv")], writes=[B("wa")])
    if debug:
        R.emit("sync", lambda s: s.dma_start(out=h_dbg_wbd[:], in_=wts_b[:]), reads=[B("wb")], dma=True)
        R.emit("sync", lambda s: s.dma_start(out=h_dbg_wz[:], in_=wts_a[:, 0:128]), reads=[B("wa")], dma=True)

    def fin_p2(t, MP, MPB):
        i = t % 4
        dst = wins[:].rearrange("p (c q) -> p c q", q=W)[:, :, i * P:(i + 1) * P]
        transpose8(MP, [MPB], dst, [B("wins"), B("wins1")])
        def womm(t_, ii=i):
            ins = None
            for j in range(2):
                sl = slice(j * 512, (j + 1) * 512)
                for c in range(NCH):
                    ins = t_.matmul(ps_b[:, sl], wins[:, c * W + ii * P:c * W + ii * P + P],
                                    wts_c[:, c * D:(c + 1) * D][:, sl], start=(c == 0), stop=(c == NCH - 1))
            return ins
        R.emit("tensor", womm, reads=[B("wins"), B("wins1"), B("wc")], writes=[B("pb")])
        R.emit("vector", lambda v, tt=t: v.tensor_tensor(ttile(ubf, tt), ttile(ubf, tt), ps_b[:], OP.add),
               reads=[B("pb"), UB(t)], writes=[UB(t)])
        R.emit("sync", lambda s, tt=t: s.dma_start(out=h_xatt[tt * P:(tt + 1) * P, :], in_=ttile(ubf, tt)),
               reads=[UB(t)], writes=[B("xatt_h")], dma=True)
        R.emit("vector", lambda v, tt=t: v.reduce_sum(stc("sx3", tt), ttile(ubf, tt), axis=AX.X),
               reads=[UB(t)], writes=[B("stats")])
        R.emit("scalar", lambda s, tt=t: s.activation(S0, ttile(ubf, tt), AF.Square, accum_out=stc("sq3", tt)),
               reads=[UB(t)], writes=[B("s0"), B("stats")])

    pend2 = None
    for t in range(NT):
        ft, ftb = FT[t % 2], FTB[t % 2]
        MP, MPB = (M0, B("m0")) if t % 2 == 0 else (M1, B("m1"))
        R.emit("sync", lambda s, tt=t, d=ft: s.dma_start(out=d, in_=h_phiq[tt]),
               reads=[B("phiq_h")], writes=[ftb], dma=True)
        def nmm(t_, d=ft):
            ins = None
            for j in range(2):
                sl = slice(j * 512, (j + 1) * 512)
                for c in range(NCH):
                    ins = t_.matmul(ps_a[:, sl], d[:, c * P:(c + 1) * P],
                                    wts_b[:, c * D:(c + 1) * D][:, sl], start=(c == 0), stop=(c == NCH - 1))
            for c in range(NCH):
                ins = t_.matmul(ps_d[:, 0:16], d[:, c * P:(c + 1) * P],
                                wts_a[:, 16 * c:16 * c + 16], start=(c == 0), stop=(c == NCH - 1))
            return ins
        R.emit("tensor", nmm, reads=[ftb, B("wb"), B("wa")], writes=[B("pa"), B("pd")])
        def divf(v, MM=MP):
            v.tensor_scalar(small[:, C_den:C_den + 16], ps_d[:, 0:16], 1e-6, None, OP.add)
            v.drain()
            v.reciprocal(small[:, C_den + 16:C_den + 32], small[:, C_den:C_den + 16])
            v.drain()
            rb = small[:, C_den + 16:C_den + 32].unsqueeze(2).to_broadcast([P, 16, DH])
            return v.tensor_tensor(MM.rearrange("p (h e) -> p h e", e=DH),
                                   ps_a[:].rearrange("p (h e) -> p h e", e=DH), rb, OP.mult)
        R.emit("vector", divf, reads=[B("pa"), B("pd")], writes=[MPB])
        if pend2 is not None:
            fin_p2(*pend2)
        pend2 = (t, MP, MPB)
    fin_p2(*pend2)
    chain_ln("sx3", "sq3", "mu3", "r3")

    for t in range(NT):
        w, i = t // 4, t % 4
        R.emit("vector", lambda v, tt=t: v.tensor_scalar(
            M0, ttile(ubf, tt), stc("mu3", tt), stc("r3", tt), OP.subtract, OP.mult),
            reads=[UB(t), B("stats")], writes=[B("m0")])
        ft, ftb = FT[t % 2], FTB[t % 2]
        transpose8(M0, [B("m0")], ft.rearrange("p (c q) -> p c q", q=P), [ftb])
        R.emit("sync", lambda s, ww=w, ii=i, d=ft: s.dma_start(
            out=h_h2t[ww].rearrange("p (c q) -> p c q", q=W)[:, :, ii * P:(ii + 1) * P],
            in_=d.rearrange("p (c q) -> p c q", q=P)),
            reads=[ftb], writes=[B("h2t_h")], dma=True)
    R.emit("vector", lambda v: v.memset(ubf[:], 0.0), writes=[UB(t) for t in range(NT)])

    ladder_s = [float(128 >> k) for k in range(8)]
    for Q in range(4):
        R.emit("gpsimd", lambda g, QQ=Q: g.dma_start(
            out=wts_a[:], in_=h_w1[:, QQ * 1024:(QQ + 1) * 1024].rearrange("(c p) n -> p c n", p=P)),
            writes=[B("wa")], dma=True)
        R.emit("gpsimd", lambda g, QQ=Q: g.dma_start(
            out=wts_b[:], in_=h_w2[QQ * 1024:(QQ + 1) * 1024, :].rearrange("(c p) n -> p c n", p=P)),
            writes=[B("wb")], dma=True)
        for c in range(NCH):
            R.emit("vector", lambda v, cc=c: v.tensor_scalar(
                wts_a[:, cc * D:(cc + 1) * D], wts_a[:, cc * D:(cc + 1) * D],
                small[:, C_g2 + cc:C_g2 + cc + 1], None, OP.mult),
                reads=[B("glob")], writes=[B("wa")])
        for j in range(2):
            def mmb(t_, jj=j):
                ins = None
                for c in range(NCH):
                    ins = t_.matmul(ps_d[0:1, 0:512], bcol[:, NCH + c:NCH + c + 1],
                                    wts_a[:, c * D + jj * 512:c * D + jj * 512 + 512],
                                    start=(c == 0), stop=(c == NCH - 1))
                return ins
            R.emit("tensor", mmb, reads=[B("wa"), B("bcol")], writes=[B("pd")])
            R.emit("vector", lambda v, jj=j: v.tensor_copy(
                brow[0:1, B_b2w1 + jj * 512:B_b2w1 + jj * 512 + 512], ps_d[0:1, 0:512]),
                reads=[B("pd")], writes=[B("br_b2w1")])

        def emit_x2(w):
            for i in range(4):
                t = 4 * w + i
                psum, pbn = (ps_a, "pa") if i % 2 == 0 else (ps_b, "pb")
                def xmm_h(t_, ii=i, ps_=psum, h0=0):
                    ins = None
                    for j in range(2):
                        sl = slice(j * 512, (j + 1) * 512)
                        for fc in range(4):
                            ffc = h0 * 4 + fc
                            ins = t_.matmul(ps_[:, sl], wins[:, ffc * W + ii * P:ffc * W + ii * P + P],
                                            wts_b[:, ffc * D:(ffc + 1) * D][:, sl],
                                            start=(ffc == 0 and j == j), stop=(ffc == NCH - 1))
                    return ins
                R.emit("tensor", lambda t_, ii=i, ps_=psum: xmm_h(t_, ii, ps_, 0),
                       reads=[B("wins"), B("wb")], writes=[B(pbn)])
                R.emit("tensor", lambda t_, ii=i, ps_=psum: xmm_h(t_, ii, ps_, 1),
                       reads=[B("wins1"), B("wb")], writes=[B(pbn)])
                R.emit("vector", lambda v, tt=t, ps_=psum: v.tensor_tensor(
                    ttile(ubf, tt), ttile(ubf, tt), ps_[:], OP.add),
                    reads=[B(pbn), UB(t)], writes=[UB(t)])

        WBUFS = [[B("xa0"), B("xa1")], [B("w1r")]]
        CBUFS = [[B("s0"), B("s1")], [B("c1r")]]
        GBUFS = [[B("m0"), B("m1")], [B("gs1")]]
        AWBUF = [B("wins"), B("wins1")]
        pend = None
        for w in range(NW):
            R.emit("sync", lambda s, ww=w: s.dma_start(out=h2w[:], in_=h_h2t[ww]),
                   reads=[B("h2t_h")], writes=[B("h2w")], dma=True)
            for half in range(2):
                for fc in range(4):
                    ffc = half * 4 + fc
                    def amm(t_, fcc=ffc):
                        ins = None
                        for c in range(NCH):
                            ins = t_.matmul(ps_d[:, 0:512], wts_a[:, c * D + fcc * P:c * D + fcc * P + P],
                                            h2w[:, c * W:(c + 1) * W], start=(c == 0), stop=False)
                        return t_.matmul(ps_d[:, 0:512],
                                         brow[0:1, B_b2w1 + fcc * P:B_b2w1 + fcc * P + P],
                                         brow[0:1, B_ones:B_ones + W], start=False, stop=True)
                    R.emit("tensor", amm, reads=[B("wa"), B("h2w"), B("br_b2w1"), B("br_ones")], writes=[B("pd")])
                    R.emit("vector", lambda v, hh=half, fcc2=fc: v.tensor_scalar(
                        WARR_H[hh][:, fcc2 * W:fcc2 * W + W], ps_d[:, 0:512], 10.0, None, OP.mult),
                        reads=[B("pd")], writes=WBUFS[half])
            if pend is not None:
                emit_x2(pend)
            for half in range(2):
                R.emit("scalar", lambda s, hh=half: s.activation(CARRY_H[hh], WARR_H[hh], AF.Identity),
                       reads=WBUFS[half], writes=CBUFS[half])
            for k in range(8):
                for half in range(2):
                    R.emit("scalar", lambda s, hh=half: s.activation(
                        GS_H[hh], CARRY_H[hh], AF.Sign, bias=small[:, C_half:C_half + 1]),
                        reads=CBUFS[half] + [B("glob")], writes=GBUFS[half])
                    R.emit("vector", lambda v, hh=half, kk=k: v.scalar_tensor_tensor(
                        CARRY_H[hh], GS_H[hh], -ladder_s[kk] / 2.0, CARRY_H[hh], OP.mult, OP.add),
                        reads=GBUFS[half] + CBUFS[half], writes=CBUFS[half])
            for half in range(2):
                R.emit("vector", lambda v, hh=half: v.tensor_tensor(
                    CARRY_H[hh], WARR_H[hh], CARRY_H[hh], OP.subtract),
                    reads=WBUFS[half] + CBUFS[half], writes=CBUFS[half])
                R.emit("scalar", lambda s, hh=half: s.activation(
                    wins[:, hh * 2048:(hh + 1) * 2048], CARRY_H[hh], AF.Gelu_apprx_tanh,
                    bias=small[:, C_gb:C_gb + 1], scale=0.1),
                    reads=CBUFS[half] + [B("glob")], writes=[AWBUF[half]])
            pend = w
        emit_x2(pend)

    for t in range(NT):
        R.emit("sync", lambda s, tt=t: s.dma_start(out=M1, in_=h_xatt[tt * P:(tt + 1) * P, :]),
               reads=[B("xatt_h")], writes=[B("m1")], dma=True)
        R.emit("vector", lambda v, tt=t: v.tensor_tensor(ttile(ubf, tt), ttile(ubf, tt), M1, OP.add),
               reads=[B("m1"), UB(t)], writes=[UB(t)])
        R.emit("vector", lambda v, tt=t: v.reduce_sum(stc("sx4", tt), ttile(ubf, tt), axis=AX.X),
               reads=[UB(t)], writes=[B("stats")])
        R.emit("scalar", lambda s, tt=t: s.activation(S0, ttile(ubf, tt), AF.Square, accum_out=stc("sq4", tt)),
               reads=[UB(t)], writes=[B("s0"), B("stats")])
    chain_ln("sx4", "sq4", "mu4", "r4")
    for t in range(NT):
        xa, xab = (XA[0], B("xa0")) if t % 2 == 0 else (XA[1], B("xa1"))
        R.emit("sync", lambda s, tt=t, d=xa: s.dma_start(out=d, in_=h_x[tt * P:(tt + 1) * P, :]),
               writes=[xab], dma=True)
        def xnf(v, tt=t, d=xa):
            v.tensor_scalar(S0, ttile(ubf, tt), stc("mu4", tt), stc("r4", tt), OP.subtract, OP.mult)
            v.drain()
            return v.tensor_tensor(ttile(ubf, tt), S0, d, OP.add)
        R.emit("vector", xnf, reads=[xab, UB(t), B("stats")], writes=[UB(t), B("s0")])
        R.emit("sync", lambda s, tt=t: s.dma_start(out=h_xnew[tt * P:(tt + 1) * P, :], in_=ttile(ubf, tt)),
               reads=[UB(t)], writes=[B("xnew_h")], dma=True)

    for t in range(NT):
        R.emit("vector", lambda v, tt=t: v.reduce_sum(stc("sx5", tt), ttile(ubf, tt), axis=AX.X),
               reads=[UB(t)], writes=[B("stats")])
        R.emit("scalar", lambda s, tt=t: s.activation(S0, ttile(ubf, tt), AF.Square, accum_out=stc("sq5", tt)),
               reads=[UB(t)], writes=[B("s0"), B("stats")])
    chain_ln("sx5", "sq5", "mu5", "r5")
    for t in range(NT):
        R.emit("vector", lambda v, tt=t: v.tensor_scalar(
            M0, ttile(ubf, tt), stc("mu5", tt), stc("r5", tt), OP.subtract, OP.mult),
            reads=[UB(t), B("stats")], writes=[B("m0")])
        ft, ftb = FT[t % 2], FTB[t % 2]
        transpose8(M0, [B("m0")], ft.rearrange("p (c q) -> p c q", q=P), [ftb])
        R.emit("sync", lambda s, tt=t, d=ft: s.dma_start(out=h_u2t[tt], in_=d),
               reads=[ftb], writes=[B("u2t_h")], dma=True)

    R.emit("gpsimd", lambda g: g.dma_start(out=wts_a[:, 0:NE * NCH],
                                           in_=h_gatet.rearrange("(c p) n -> p c n", p=P)),
           writes=[B("wa")], dma=True)
    for t in range(NT):
        ft, ftb = FT[t % 2], FTB[t % 2]
        R.emit("sync", lambda s, tt=t, d=ft: s.dma_start(out=d, in_=h_u2t[tt]),
               reads=[B("u2t_h")], writes=[ftb], dma=True)
        def lmm(t_, d=ft):
            ins = None
            for c in range(NCH):
                ins = t_.matmul(ps_d[:, 0:NE], d[:, c * P:(c + 1) * P], wts_a[:, NE * c:NE * c + NE],
                                start=(c == 0), stop=(c == NCH - 1))
            return ins
        R.emit("tensor", lmm, reads=[ftb, B("wa")], writes=[B("pd")])
        R.emit("vector", lambda v, tt=t: v.tensor_copy(small[:, C_logit + NE * tt:C_logit + NE * tt + NE],
                                                       ps_d[:, 0:NE]),
               reads=[B("pd")], writes=[B("rt")])
    LG = small[:, C_logit:C_logit + 256]
    SC = small[:, C_wts:C_wts + 256]
    MSK = small[:, C_msk:C_msk + 256]
    RS = small[:, C_rsc:C_rsc + NT]
    r3d = lambda ap: ap.rearrange("p (t e) -> p t e", e=NE)
    rb3 = lambda ap: ap.rearrange("p (t o) -> p t o", o=1).to_broadcast([P, NT, NE])
    rt_emit = lambda fn: R.emit("vector", fn, reads=[B("rt")], writes=[B("rt")])
    rt_emit(lambda v: v.reduce_max(RS.rearrange("p (t o) -> p t o", o=1), r3d(LG), axis=AX.X))
    rt_emit(lambda v: v.tensor_tensor(r3d(SC), r3d(LG), rb3(RS), OP.subtract))
    R.emit("scalar", lambda s: s.activation(SC, SC, AF.Exp), reads=[B("rt")], writes=[B("rt")])
    RS2 = small[:, C_rsc2:C_rsc2 + NT]
    def rsum(v):
        v.reduce_sum(RS.rearrange("p (t o) -> p t o", o=1), r3d(SC), axis=AX.X)
        v.drain()
        v.reciprocal(RS2, RS)
        return v.drain()
    rt_emit(rsum)
    rb32 = RS2.rearrange("p (t o) -> p t o", o=1).to_broadcast([P, NT, NE])
    rt_emit(lambda v: v.tensor_tensor(r3d(SC), r3d(SC), rb32, OP.mult))
    rt_emit(lambda v: v.reduce_max(RS.rearrange("p (t o) -> p t o", o=1), r3d(SC), axis=AX.X))
    rt_emit(lambda v: v.tensor_tensor(r3d(MSK), r3d(SC), rb3(RS), OP.is_ge))
    rt_emit(lambda v: v.tensor_tensor(LG, SC, MSK, OP.mult))
    rt_emit(lambda v: v.tensor_tensor(LG, SC, LG, OP.subtract))
    rt_emit(lambda v: v.reduce_max(RS.rearrange("p (t o) -> p t o", o=1), r3d(LG), axis=AX.X))
    def m2f(v):
        v.tensor_tensor(r3d(LG), r3d(LG), rb3(RS), OP.is_ge)
        v.drain()
        v.tensor_tensor(MSK, MSK, LG, OP.add)
        v.drain()
        return v.tensor_tensor(SC, SC, MSK, OP.mult)
    rt_emit(m2f)

    R.emit("vector", lambda v: v.memset(ubf[:], 0.0), writes=[UB(t) for t in range(NT)])

    for e in range(NE):
        R.emit("gpsimd", lambda g, ee=e: g.dma_start(out=wts_a[:],
                                                     in_=h_mw1t[ee].rearrange("(c p) n -> p c n", p=P)),
               writes=[B("wa")], dma=True)
        R.emit("gpsimd", lambda g, ee=e: g.dma_start(out=wts_b[:],
                                                     in_=h_mw2t[ee].rearrange("(c p) n -> p c n", p=P)),
               writes=[B("wb")], dma=True)
        R.emit("gpsimd", lambda g, ee=e: g.dma_start(out=wts_c[:],
                                                     in_=h_mw3t[ee].rearrange("(c p) n -> p c n", p=P)),
               writes=[B("wc")], dma=True)
        R.emit("gpsimd", lambda g, ee=e: g.dma_start(out=betile[:, 0:D], in_=h_be1[ee]),
               writes=[B("betile")], dma=True)
        R.emit("gpsimd", lambda g, ee=e: g.dma_start(out=betile[:, D:2 * D], in_=h_be3[ee]),
               reads=[B("betile")], writes=[B("betile")], dma=True)
        R.emit("gpsimd", lambda g, ee=e: g.dma_start(out=brow[0:1, B_b2e:B_b2e + D], in_=h_be2r[ee]),
               writes=[B("br_b2e")], dma=True)
        HB = [(M0, B("m0")), (h2w[:, 0:1024], B("h2w"))]
        XB3 = [(M1, B("m1")), (h2w[:, 1024:2048], B("h2w"))]
        SB_ = [(S0, B("s0")), (S1, B("s1"))]
        PRB = [(XA[0], B("xa0")), (XA[1], B("xa1"))]
        WSL = [wins[:, 0:1024], wins[:, 1024:2048]]

        def fin_moe(t, mcol):
            pp = t % 2
            hb, hbb = HB[pp]
            xb, xbb = XB3[pp]
            prb, prbb = PRB[pp]
            def tfh(t_, src=hb):
                ins = None
                for c in range(NCH):
                    ins = t_.transpose(ps_t[:, c * P:(c + 1) * P], src[:, c * P:(c + 1) * P], ident[:])
                return ins
            R.emit("tensor", tfh, reads=[hbb, B("ident")], writes=[B("pt")])
            R.emit("vector", lambda v, ws=WSL[pp]: v.tensor_copy(ws, ps_t[:]),
                   reads=[B("pt")], writes=[B("wins")])
            def x2mm(t_, ws=WSL[pp]):
                ins = None
                for j in range(2):
                    sl = slice(j * 512, (j + 1) * 512)
                    for c in range(NCH):
                        ins = t_.matmul(ps_c[:, sl], ws[:, c * P:(c + 1) * P],
                                        wts_b[:, c * D:(c + 1) * D][:, sl], start=(c == 0), stop=False)
                    ins = t_.matmul(ps_c[:, sl], brow[0:1, B_ones:B_ones + P],
                                    brow[0:1, B_b2e:B_b2e + D][:, sl], start=False, stop=True)
                return ins
            R.emit("tensor", x2mm, reads=[B("wins"), B("wb"), B("br_b2e"), B("br_ones")], writes=[B("pc")])
            R.emit("vector", lambda v, mc=mcol, xb_=xb, pr=prb: v.scalar_tensor_tensor(
                pr, ps_c[:], mc, xb_, OP.mult, OP.mult),
                reads=[B("pc"), xbb, B("rt")], writes=[prbb])
            R.emit("vector", lambda v, tt=t, pr=prb: v.tensor_tensor(ttile(ubf, tt), ttile(ubf, tt), pr, OP.add),
                   reads=[prbb, UB(t)], writes=[UB(t)])

        pend4 = None
        for t in range(NT):
            pp = t % 2
            ft, ftb = FT[pp], FTB[pp]
            R.emit("sync", lambda s, tt=t, d=ft: s.dma_start(out=d, in_=h_u2t[tt]),
                   reads=[B("u2t_h")], writes=[ftb], dma=True)
            wcol = small[:, C_wts + NE * t + e:C_wts + NE * t + e + 1]
            mcol = small[:, C_msk + NE * t + e:C_msk + NE * t + e + 1]
            def pmm(t_, d, wbuf, ps_):
                ins = None
                for j in range(2):
                    sl = slice(j * 512, (j + 1) * 512)
                    for c in range(NCH):
                        ins = t_.matmul(ps_[:, sl], d[:, c * P:(c + 1) * P],
                                        wbuf[:, c * D:(c + 1) * D][:, sl],
                                        start=(c == 0), stop=(c == NCH - 1))
                return ins
            hb, hbb = HB[pp]
            xb, xbb = XB3[pp]
            sb_, sbb = SB_[pp]
            R.emit("tensor", lambda t_, d=ft: pmm(t_, d, wts_a, ps_a),
                   reads=[ftb, B("wa")], writes=[B("pa")])
            R.emit("tensor", lambda t_, d=ft: pmm(t_, d, wts_c, ps_b),
                   reads=[ftb, B("wc")], writes=[B("pb")])
            R.emit("vector", lambda v, wc=wcol, s_=sb_: v.scalar_tensor_tensor(
                s_, ps_a[:], wc, betile[:, 0:D], OP.mult, OP.add),
                reads=[B("pa"), B("betile"), B("rt")], writes=[sbb])
            R.emit("scalar", lambda s, s_=sb_, h_=hb: s.activation(h_, s_, AF.Silu),
                   reads=[sbb], writes=[hbb])
            R.emit("vector", lambda v, wc=wcol, x_=xb: v.scalar_tensor_tensor(
                x_, ps_b[:], wc, betile[:, D:2 * D], OP.mult, OP.add),
                reads=[B("pb"), B("betile"), B("rt")], writes=[xbb])
            if pend4 is not None:
                fin_moe(*pend4)
            pend4 = (t, mcol)
        fin_moe(*pend4)
        pend4 = None

    for hf in range(2):
        R.emit("gpsimd", lambda g, h_=hf: g.dma_start(
            out=wts_a[:], in_=h_sw1t[:, h_ * 1024:(h_ + 1) * 1024].rearrange("(c p) n -> p c n", p=P)),
            writes=[B("wa")], dma=True)
        R.emit("gpsimd", lambda g, h_=hf: g.dma_start(
            out=wts_b[:], in_=h_sw2t[:, h_ * 1024:(h_ + 1) * 1024].rearrange("(c p) n -> p c n", p=P)),
            writes=[B("wb")], dma=True)
        R.emit("gpsimd", lambda g, h_=hf: g.dma_start(
            out=wts_c[:], in_=h_sw3t[h_ * 1024:(h_ + 1) * 1024, :].rearrange("(c p) n -> p c n", p=P)),
            writes=[B("wc")], dma=True)
        for t in range(NT):
            ft, ftb = FT[t % 2], FTB[t % 2]
            R.emit("sync", lambda s, tt=t, d=ft: s.dma_start(out=d, in_=h_u2t[tt]),
                   reads=[B("u2t_h")], writes=[ftb], dma=True)
            def gmm(t_, d, wbuf, ps_):
                ins = None
                for j in range(2):
                    sl = slice(j * 512, (j + 1) * 512)
                    for c in range(NCH):
                        ins = t_.matmul(ps_[:, sl], d[:, c * P:(c + 1) * P],
                                        wbuf[:, c * D:(c + 1) * D][:, sl],
                                        start=(c == 0), stop=(c == NCH - 1))
                return ins
            R.emit("tensor", lambda t_, d=ft: gmm(t_, d, wts_a, ps_a),
                   reads=[ftb, B("wa")], writes=[B("pa")])
            R.emit("tensor", lambda t_, d=ft: gmm(t_, d, wts_b, ps_b),
                   reads=[ftb, B("wb")], writes=[B("pb")])
            R.emit("scalar", lambda s: s.activation(S0, ps_a[:], AF.Silu), reads=[B("pa")], writes=[B("s0")])
            R.emit("vector", lambda v: v.tensor_tensor(M0, S0, ps_b[:], OP.mult),
                   reads=[B("s0"), B("pb")], writes=[B("m0")])
            transpose8(M0, [B("m0")], wins[:, 0:1024].rearrange("p (c q) -> p c q", q=P), [B("wins")])
            def zmm(t_):
                ins = None
                for j in range(2):
                    sl = slice(j * 512, (j + 1) * 512)
                    for c in range(NCH):
                        ins = t_.matmul(ps_c[:, sl], wins[:, c * P:(c + 1) * P],
                                        wts_c[:, c * D:(c + 1) * D][:, sl], start=(c == 0), stop=(c == NCH - 1))
                return ins
            R.emit("tensor", zmm, reads=[B("wins"), B("wc")], writes=[B("pc")])
            R.emit("vector", lambda v, tt=t: v.tensor_tensor(ttile(ubf, tt), ttile(ubf, tt), ps_c[:], OP.add),
                   reads=[B("pc"), UB(t)], writes=[UB(t)])

    for t in range(NT):
        R.emit("vector", lambda v, tt=t: v.reduce_sum(stc("sx6", tt), ttile(ubf, tt), axis=AX.X),
               reads=[UB(t)], writes=[B("stats")])
        R.emit("scalar", lambda s, tt=t: s.activation(S0, ttile(ubf, tt), AF.Square, accum_out=stc("sq6", tt)),
               reads=[UB(t)], writes=[B("s0"), B("stats")])
    chain_ln("sx6", "sq6", "mu6", "r6")
    for t in range(NT):
        xa, xab = (XA[0], B("xa0")) if t % 2 == 0 else (XA[1], B("xa1"))
        mb, mbb = (M0, B("m0")) if t % 2 == 0 else (M1, B("m1"))
        R.emit("sync", lambda s, tt=t, mm=mb: s.dma_start(out=mm, in_=h_xnew[tt * P:(tt + 1) * P, :]),
               reads=[B("xnew_h")], writes=[mbb], dma=True)
        def fin(v, tt=t, d=xa, mm=mb):
            v.tensor_scalar(S0, ttile(ubf, tt), stc("mu6", tt), stc("r6", tt), OP.subtract, OP.mult)
            v.drain()
            return v.tensor_tensor(d, S0, mm, OP.add)
        R.emit("vector", fin, reads=[mbb, UB(t), B("stats")], writes=[xab, B("s0")])
        R.emit("sync", lambda s, tt=t, d=xa: s.dma_start(out=h_out[tt * P:(tt + 1) * P, :], in_=d),
               reads=[xab], dma=True)
    if debug:
        R.emit("sync", lambda s: s.dma_start(out=h_dbg_small[:], in_=small[:]),
               reads=[B("stats"), B("rt"), B("kv"), B("glob")], dma=True)
    fin_waits = [(s, R.v(s)) for s in DMA_POOLS["sync"] if R.v(s) > 0]
    R.items["sync"].append((lambda s: s.nop(), fin_waits, None, 0))

    if n_instr_out is not None:
        n_instr_out.update({e: len(R.items[e]) for e in ENGINES})

    with es:
        with nc.Block() as block:
            def mk(eng_name):
                def section(eng):
                    for fn, waits, inc, inc_n in R.items[eng_name]:
                        for s, val in waits:
                            eng.wait_ge(sems[s], val)
                        ins = fn(eng)
                        if inc is not None and inc_n > 0:
                            if inc_n == 16:
                                assert ins is not None, f"{eng_name}: dma inc but fn returned None"
                                ins.then_inc(sems[inc], inc_n)
                            else:
                                eng.drain().then_inc(sems[inc], inc_n)
                return section
            block.sync(mk("sync"))
            block.gpsimd(mk("gpsimd"))
            block.vector(mk("vector"))
            block.scalar(mk("scalar"))
            block.tensor(mk("tensor"))
    return nc


def prep_inputs(inp):
    f = np.float32
    asf = lambda a: np.asarray(a, f)
    x = asf(inp["L0_attn_x"])
    col = lambda v: np.ascontiguousarray(asf(v).reshape(NCH, P).T)
    mb1, mb2, mb3 = asf(inp["L0_ffn_b1"]), asf(inp["L0_ffn_b2"]), asf(inp["L0_ffn_b3"])
    shared = {
        "wq": np.ascontiguousarray(asf(inp["L0_attn_wq"])),
        "wk": np.ascontiguousarray(asf(inp["L0_attn_wk"])),
        "wv": np.ascontiguousarray(asf(inp["L0_attn_wv"])),
        "wo": np.ascontiguousarray(asf(inp["L0_attn_wo"])),
        "g1c": col(inp["L0_attn_ln1_g"]), "b1c": col(inp["L0_attn_ln1_b"]),
        "g2c": col(inp["L0_attn_ln2_g"]), "b2c": col(inp["L0_attn_ln2_b"]),
        "w1": np.ascontiguousarray(asf(inp["L0_attn_w1"])),
        "w2": np.ascontiguousarray(asf(inp["L0_attn_w2"])),
        "gatet": np.ascontiguousarray(asf(inp["L0_ffn_gate_w"]).T),
        "mw1t": np.ascontiguousarray(asf(inp["L0_ffn_w1"]).transpose(0, 2, 1)),
        "mw2t": np.ascontiguousarray(asf(inp["L0_ffn_w2"]).transpose(0, 2, 1)),
        "mw3t": np.ascontiguousarray(asf(inp["L0_ffn_w3"]).transpose(0, 2, 1)),
        "be1": np.ascontiguousarray(np.broadcast_to(mb1[:, None, :], (NE, P, D))),
        "be3": np.ascontiguousarray(np.broadcast_to(mb3[:, None, :], (NE, P, D))),
        "be2r": np.ascontiguousarray(mb2[:, None, :]),
        "sw1t": np.ascontiguousarray(asf(inp["L0_ffn_sw1"]).T),
        "sw2t": np.ascontiguousarray(asf(inp["L0_ffn_sw2"]).T),
        "sw3t": np.ascontiguousarray(asf(inp["L0_ffn_sw3"]).T),
        "ident": np.eye(P, dtype=f),
        "onesrow": np.ones((1, W), dtype=f),
    }
    return [dict(shared, x=np.ascontiguousarray(x[c])) for c in range(x.shape[0])]


_NC_CACHE = [None]


def kernel(**inputs):
    in_maps = prep_inputs(inputs)
    if _NC_CACHE[0] is None:
        _NC_CACHE[0] = build()
    res = run_bass_kernel_spmd(_NC_CACHE[0], in_maps, core_ids=list(range(len(in_maps))))
    return np.stack([np.asarray(r["out"], np.float32) for r in res.results], axis=0)


# version 11
# speedup vs baseline: 1.0210x; 1.0210x over previous
import contextlib
import numpy as np
import concourse.bass as bass
import concourse.mybir as mybir
from concourse.bass_utils import run_bass_kernel_spmd

F32 = mybir.dt.float32
BF16 = mybir.dt.bfloat16
AF = mybir.ActivationFunctionType
OP = mybir.AluOpType
AX = mybir.AxisListType

P, D, T = 128, 1024, 4096
NT, W, NW, NCH = 32, 512, 8, 8
H, DH, FF, NE, I2 = 16, 64, 4096, 8, 2048
EPS = 1e-5

ENGINES = ("sync", "gpsimd", "vector", "scalar", "tensor")
OWN = {"vector": "sV", "scalar": "sS", "tensor": "sT", "gpsimd": "sG", "sync": None}


class Buf:
    __slots__ = ("w", "r")

    def __init__(self):
        self.w = None
        self.r = []


DMA_POOLS = {"sync": ["sX%d" % i for i in range(10)],
             "gpsimd": ["sW%d" % i for i in range(6)]}


class Rec:
    def __init__(self):
        self.items = {e: [] for e in ENGINES}
        self.cnt = {}
        self.guard = {}
        self.dma_rr = {e: 0 for e in DMA_POOLS}

    def v(self, sem):
        return self.cnt.get(sem, 0)

    def emit(self, eng, fn, reads=(), writes=(), extra=(), dma=None):
        waits = {}
        for b in reads:
            if b.w is not None:
                s, v = b.w
                waits[s] = max(waits.get(s, 0), v)
        for b in writes:
            for dep in ([b.w] if b.w else []) + b.r:
                s, v = dep
                waits[s] = max(waits.get(s, 0), v)
        for s, v in extra:
            waits[s] = max(waits.get(s, 0), v)
        own = OWN[eng]
        if dma is not None:
            pool = DMA_POOLS[eng]
            sem = pool[self.dma_rr[eng] % len(pool)]
            self.dma_rr[eng] += 1
            prev = self.cnt.get(sem, 0)
            if prev > 0:
                waits[sem] = max(waits.get(sem, 0), prev)
            n = 16
        else:
            sem, n = own, 1
        w2 = []
        for s, v in waits.items():
            if v <= 0 or s == own:
                continue
            k = (eng, s)
            if self.guard.get(k, 0) >= v:
                continue
            self.guard[k] = v
            w2.append((s, v))
        assert sem is not None
        self.items[eng].append((fn, w2, sem, n))
        self.cnt[sem] = self.cnt.get(sem, 0) + n
        val = self.cnt[sem]
        for b in writes:
            b.w = (sem, val)
            b.r = []
        for b in reads:
            b.r.append((sem, val))
        return sem, val


def build(n_instr_out=None, debug=False):
    nc = bass.Bass("TRN2", target_bir_lowering=False)

    h_x = nc.dram_tensor("x", [T, D], F32, kind="ExternalInput").ap()
    h_wq = nc.dram_tensor("wq", [D, D], F32, kind="ExternalInput").ap()
    h_wk = nc.dram_tensor("wk", [D, D], F32, kind="ExternalInput").ap()
    h_wv = nc.dram_tensor("wv", [D, D], F32, kind="ExternalInput").ap()
    h_wo = nc.dram_tensor("wo", [D, D], F32, kind="ExternalInput").ap()
    h_g1c = nc.dram_tensor("g1c", [P, NCH], F32, kind="ExternalInput").ap()
    h_b1c = nc.dram_tensor("b1c", [P, NCH], F32, kind="ExternalInput").ap()
    h_g2c = nc.dram_tensor("g2c", [P, NCH], F32, kind="ExternalInput").ap()
    h_b2c = nc.dram_tensor("b2c", [P, NCH], F32, kind="ExternalInput").ap()
    h_w1 = nc.dram_tensor("w1", [D, FF], F32, kind="ExternalInput").ap()
    h_w2 = nc.dram_tensor("w2", [FF, D], F32, kind="ExternalInput").ap()
    h_gatet = nc.dram_tensor("gatet", [D, NE], F32, kind="ExternalInput").ap()
    h_mw1t = nc.dram_tensor("mw1t", [NE, D, D], F32, kind="ExternalInput").ap()
    h_mw2t = nc.dram_tensor("mw2t", [NE, D, D], F32, kind="ExternalInput").ap()
    h_mw3t = nc.dram_tensor("mw3t", [NE, D, D], F32, kind="ExternalInput").ap()
    h_be1 = nc.dram_tensor("be1", [NE, P, D], F32, kind="ExternalInput").ap()
    h_be3 = nc.dram_tensor("be3", [NE, P, D], F32, kind="ExternalInput").ap()
    h_be2r = nc.dram_tensor("be2r", [NE, 1, D], F32, kind="ExternalInput").ap()
    h_sw1t = nc.dram_tensor("sw1t", [D, I2], F32, kind="ExternalInput").ap()
    h_sw2t = nc.dram_tensor("sw2t", [D, I2], F32, kind="ExternalInput").ap()
    h_sw3t = nc.dram_tensor("sw3t", [I2, D], F32, kind="ExternalInput").ap()
    h_ident = nc.dram_tensor("ident", [P, P], F32, kind="ExternalInput").ap()
    h_ones = nc.dram_tensor("onesrow", [1, W], F32, kind="ExternalInput").ap()
    h_out = nc.dram_tensor("out", [T, D], F32, kind="ExternalOutput").ap()
    KI = "ExternalOutput"
    if debug:
        h_dbg_wbd = nc.dram_tensor("dbg_wbd", [P, 8192], BF16, kind="ExternalOutput").ap()
        h_dbg_wz = nc.dram_tensor("dbg_wz", [P, 128], BF16, kind="ExternalOutput").ap()
        h_dbg_small = nc.dram_tensor("dbg_small", [P, 2176], F32, kind="ExternalOutput").ap()
    h_xatt = nc.dram_tensor("sp_xatt", [T, D], BF16, kind=KI).ap()
    h_xnew = nc.dram_tensor("sp_xnew", [T, D], BF16, kind=KI).ap()
    h_phiq = nc.dram_tensor("sp_phiq", [NT, P, D], BF16, kind=KI).ap()
    h_h2t = nc.dram_tensor("sp_h2t", [NW, P, NCH * W], BF16, kind=KI).ap()
    h_u2t = nc.dram_tensor("sp_u2t", [NT, P, D], BF16, kind=KI).ap()

    es = contextlib.ExitStack()
    sbt = lambda name, shape, dt: es.enter_context(nc.sbuf_tensor(name, shape, dt))

    ubf = sbt("ubf", [P, NT * D], BF16)
    wts_a = sbt("wts_a", [P, 8192], BF16)
    wts_b = sbt("wts_b", [P, 8192], BF16)
    wts_c = sbt("wts_c", [P, 8192], BF16)
    f32a = sbt("f32a", [P, 8192], F32)
    small = sbt("small", [P, 2176], F32)
    wins = sbt("wins", [P, 4096], BF16)
    misc = sbt("misc", [P, 5184], BF16)
    h2w = sbt("h2w", [P, 4096], BF16)
    ftile = sbt("ftile", [P, 2 * D], BF16)
    pstage = sbt("pstage", [P, 2 * W], BF16)
    brow = sbt("brow", [1, 8704], BF16)
    bcol = sbt("bcol", [P, 2 * NCH], BF16)
    ident = sbt("identt", [P, P], BF16)
    betile = sbt("betile", [P, 2 * D], BF16)

    ps_a = es.enter_context(nc.psum_tensor("ps_a", [P, 1024], F32))
    ps_b = es.enter_context(nc.psum_tensor("ps_b", [P, 1024], F32))
    ps_c = es.enter_context(nc.psum_tensor("ps_c", [P, 1024], F32))
    ps_d = es.enter_context(nc.psum_tensor("ps_d", [P, 512], F32))
    ps_t = es.enter_context(nc.psum_tensor("ps_t", [P, 1024], BF16))

    sems = {}
    for s in ["sV", "sS", "sT", "sG"] + DMA_POOLS["sync"] + DMA_POOLS["gpsimd"]:
        sems[s] = es.enter_context(nc.semaphore(s))

    R = Rec()

    ST = {}
    off = 0
    for name in ("sx1", "sq1", "mu1", "r1", "rr2", "sx3", "sq3", "mu3", "r3",
                 "sx4", "sq4", "mu4", "r4", "sx5", "sq5", "mu5", "r5",
                 "sx6", "sq6", "mu6", "r6"):
        ST[name] = off; off += NT
    C_scr = off; off += 2 * NT
    C_g1 = off; off += NCH
    C_g2 = off; off += NCH
    C_half = off; off += 1
    C_gb = off; off += 1
    C_eps = off; off += 1
    C_kv = off; off += 520
    C_logit = off; off += 256
    C_wts = off; off += 256
    C_msk = off; off += 256
    C_rsc = off; off += NT
    C_den = off; off += 32
    C_spc = off; off += 1
    C_rsc2 = off; off += NT
    assert off <= 2176, off

    def stc(name, t):
        return small[:, ST[name] + t:ST[name] + t + 1]

    def strow(name):
        return small[:, ST[name]:ST[name] + NT]

    XA = [f32a[:, 0:1024], f32a[:, 1024:2048]]
    S0 = f32a[:, 2048:3072]
    S1 = f32a[:, 3072:4096]
    WARR = f32a[:, 0:2048]
    CARRY = f32a[:, 2048:4096]

    M0 = misc[:, 0:1024]
    M1 = misc[:, 1024:2048]
    V65 = misc[:, 2048:3088]
    GS_H = [misc[:, 0:2048], misc[:, 3136:5184]]
    WARR_H = [f32a[:, 0:2048], f32a[:, 4096:6144]]
    CARRY_H = [f32a[:, 2048:4096], f32a[:, 6144:8192]]

    B_wq, B_wk, B_wv, B_b2w1, B_ones, B_b2e = 0, 1024, 2048, 3072, 7168, 7680

    bufs = {}
    for t in range(NT):
        bufs["ubf%d" % t] = Buf()
    for n in ("wa", "wb", "wc", "xa0", "xa1", "s0", "s1", "wins", "m0", "m1",
              "v65", "kv", "stats", "rt", "br_q", "br_k", "br_v",
              "br_b2w1", "br_ones", "br_b2e", "bcol", "ident", "betile", "h2w",
              "ft0", "ft1", "pg0", "pg1", "w1r", "c1r", "gs1", "wins1",
              "pa", "pb", "pc", "pd", "pt",
              "xatt_h", "xnew_h", "phiq_h", "h2t_h", "u2t_h", "glob"):
        bufs[n] = Buf()
    B = lambda n: bufs[n]
    UB = lambda t: bufs["ubf%d" % t]

    FT = [ftile[:, 0:D], ftile[:, D:2 * D]]
    FTB = [B("ft0"), B("ft1")]
    PG = [pstage[:, 0:W], pstage[:, W:2 * W]]
    PGB = [B("pg0"), B("pg1")]

    def ttile(buf, t):
        return buf[:, t * D:(t + 1) * D]

    R.emit("gpsimd", lambda g: g.dma_start(out=ident[:], in_=h_ident[:]), writes=[B("ident")], dma=True)
    R.emit("gpsimd", lambda g: g.dma_start(out=bcol[:, 0:NCH], in_=h_b1c[:]), writes=[B("bcol")], dma=True)
    R.emit("gpsimd", lambda g: g.dma_start(out=bcol[:, NCH:2 * NCH], in_=h_b2c[:]), reads=[B("bcol")],
           writes=[B("bcol")], dma=True)
    R.emit("gpsimd", lambda g: g.dma_start(out=brow[0:1, B_ones:B_ones + W], in_=h_ones[:]),
           writes=[B("br_ones")], dma=True)
    for dst, src, bn in ((wts_a, h_wq, "wa"), (wts_b, h_wk, "wb"), (wts_c, h_wv, "wc")):
        R.emit("gpsimd", lambda g, d=dst, s=src: g.dma_start(out=d[:], in_=s.rearrange("(c p) n -> p c n", p=P)),
               writes=[B(bn)], dma=True)
    R.emit("sync", lambda s: s.dma_start(out=small[:, C_g1:C_g1 + NCH], in_=h_g1c[:]), writes=[B("glob")], dma=True)
    R.emit("sync", lambda s: s.dma_start(out=small[:, C_g2:C_g2 + NCH], in_=h_g2c[:]), reads=[B("glob")],
           writes=[B("glob")], dma=True)
    R.emit("vector", lambda v: v.memset(small[:, C_half:C_half + 1], 0.5), writes=[B("glob")])
    R.emit("vector", lambda v: v.memset(small[:, C_gb:C_gb + 1], -0.05), writes=[B("glob")])
    R.emit("vector", lambda v: v.memset(small[:, C_eps:C_eps + 1], EPS), writes=[B("glob")])
    R.emit("vector", lambda v: v.memset(small[:, C_kv:C_kv + 520], 0.0), writes=[B("kv")])
    R.emit("vector", lambda v: v.memset(V65, 1.0), writes=[B("v65")])

    for bn, wbuf in (("wa", wts_a), ("wb", wts_b), ("wc", wts_c)):
        for c in range(NCH):
            R.emit("vector", lambda v, wb=wbuf, cc=c: v.tensor_scalar(
                wb[:, cc * D:(cc + 1) * D], wb[:, cc * D:(cc + 1) * D],
                small[:, C_g1 + cc:C_g1 + cc + 1], None, OP.mult),
                reads=[B("glob")], writes=[B(bn)])

    for wbuf, bn, bofs, brn in ((wts_a, "wa", B_wq, "br_q"), (wts_b, "wb", B_wk, "br_k"),
                                (wts_c, "wc", B_wv, "br_v")):
        for j in range(2):
            def mmf(t, wb=wbuf, jj=j):
                ins = None
                for c in range(NCH):
                    ins = t.matmul(ps_d[0:1, 0:512], bcol[:, c:c + 1],
                                   wb[:, c * D + jj * 512:c * D + jj * 512 + 512],
                                   start=(c == 0), stop=(c == NCH - 1))
                return ins
            R.emit("tensor", mmf, reads=[B(bn), B("bcol")], writes=[B("pd")])
            R.emit("vector", lambda v, bo=bofs, jj=j: v.tensor_copy(
                brow[0:1, bo + jj * 512:bo + jj * 512 + 512], ps_d[0:1, 0:512]),
                reads=[B("pd")], writes=[B(brn)])

    for t in range(NT):
        xa, xab = (XA[0], B("xa0")) if t % 2 == 0 else (XA[1], B("xa1"))
        R.emit("sync", lambda s, tt=t, d=xa: s.dma_start(out=d, in_=h_x[tt * P:(tt + 1) * P, :]),
               writes=[xab], dma=True)
        R.emit("vector", lambda v, tt=t, d=xa: v.reduce_sum(stc("sx1", tt), d, axis=AX.X),
               reads=[xab], writes=[B("stats")])
        R.emit("scalar", lambda s, tt=t, d=xa: s.activation(S0, d, AF.Square, accum_out=stc("sq1", tt)),
               reads=[xab], writes=[B("s0"), B("stats")])

    def chain_ln(sx, sq, mu, r, rr2=False):
        scr0 = small[:, C_scr:C_scr + NT]
        scr1 = small[:, C_scr + NT:C_scr + 2 * NT]
        def fn(v):
            v.tensor_scalar(strow(mu), strow(sx), 1.0 / D, None, OP.mult)
            v.tensor_scalar(scr0, strow(sq), 1.0 / D, None, OP.mult)
            v.drain()
            v.tensor_tensor(scr1, strow(mu), strow(mu), OP.mult)
            v.drain()
            return v.tensor_tensor(scr0, scr0, scr1, OP.subtract)
        R.emit("vector", fn, reads=[B("stats")], writes=[B("stats")])
        R.emit("scalar", lambda s: s.activation(scr1, scr0, AF.Sqrt, bias=small[:, C_eps:C_eps + 1]),
               reads=[B("stats"), B("glob")], writes=[B("stats")])
        def recf(v):
            v.reciprocal(strow(r), scr1)
            return v.drain()
        R.emit("vector", recf, reads=[B("stats")], writes=[B("stats")])
        if rr2:
            def fn2(v):
                v.tensor_tensor(scr1, strow(r), strow(r), OP.mult)
                v.drain()
                return v.tensor_tensor(scr1, scr1, scr0, OP.mult)
            R.emit("vector", fn2, reads=[B("stats")], writes=[B("stats")])
            R.emit("scalar", lambda s: s.activation(scr1, scr1, AF.Sqrt, bias=small[:, C_eps:C_eps + 1]),
                   reads=[B("stats"), B("glob")], writes=[B("stats")])
            def fn3(v):
                v.reciprocal(scr0, scr1)
                v.drain()
                return v.tensor_tensor(strow("rr2"), scr0, strow(r), OP.mult)
            R.emit("vector", fn3, reads=[B("stats")], writes=[B("stats")])

    chain_ln("sx1", "sq1", "mu1", "r1", rr2=True)

    def transpose8(src_ap, src_bufs, dst_ap, dst_bufs, evac="vector"):
        def tf(t_):
            ins = None
            for c in range(NCH):
                ins = t_.transpose(ps_t[:, c * P:(c + 1) * P], src_ap[:, c * P:(c + 1) * P], ident[:])
            return ins
        R.emit("tensor", tf, reads=list(src_bufs) + [B("ident")], writes=[B("pt")])
        if evac == "scalar":
            R.emit(evac, lambda s: s.copy(dst_ap, ps_t[:].rearrange("p (c q) -> p c q", q=P)),
                   reads=[B("pt")], writes=list(dst_bufs))
        else:
            R.emit(evac, lambda v: v.tensor_copy(dst_ap, ps_t[:].rearrange("p (c q) -> p c q", q=P)),
                   reads=[B("pt")], writes=list(dst_bufs))

    for w in range(NW):
        w0 = w * W
        for i in range(4):
            t = 4 * w + i
            xa, xab = (XA[0], B("xa0")) if t % 2 == 0 else (XA[1], B("xa1"))
            R.emit("sync", lambda s, tt=t, d=xa: s.dma_start(out=d, in_=h_x[tt * P:(tt + 1) * P, :]),
                   writes=[xab], dma=True)
            R.emit("vector", lambda v, tt=t, d=xa: v.tensor_scalar(
                ttile(ubf, tt), d, stc("mu1", tt), stc("r1", tt), OP.subtract, OP.mult),
                reads=[xab, B("stats")], writes=[UB(t)])
            R.emit("vector", lambda v, tt=t, d=xa: v.tensor_scalar(
                M0, d, stc("mu1", tt), stc("rr2", tt), OP.subtract, OP.mult),
                reads=[xab, B("stats")], writes=[B("m0")])
            dst = wins[:].rearrange("p (c q) -> p c q", q=W)[:, :, i * P:(i + 1) * P]
            transpose8(M0, [B("m0")], dst, [B("wins"), B("wins1")])

        for co in range(NCH):
            pg, pgb = PG[co % 2], PGB[co % 2]
            def qmm(t_, cc=co):
                ins = None
                for c in range(NCH):
                    ins = t_.matmul(ps_d[:, 0:512], wts_a[:, c * D + cc * P:c * D + cc * P + P],
                                    wins[:, c * W:(c + 1) * W], start=(c == 0), stop=False)
                return t_.matmul(ps_d[:, 0:512], brow[0:1, B_wq + cc * P:B_wq + cc * P + P],
                                 brow[0:1, B_ones:B_ones + W], start=False, stop=True)
            R.emit("tensor", qmm, reads=[B("wins"), B("wins1"), B("wa"), B("br_q"), B("br_ones")], writes=[B("pd")])
            R.emit("vector", lambda v: v.tensor_scalar(S0[:, 0:W], ps_d[:, 0:512], 0.0, None, OP.min),
                   reads=[B("pd")], writes=[B("s0")])
            R.emit("scalar", lambda s: s.activation(S1[:, 0:W], S0[:, 0:W], AF.Exp),
                   reads=[B("s0")], writes=[B("s1")])
            R.emit("vector", lambda v, p_=pg: v.scalar_tensor_tensor(
                p_, ps_d[:, 0:512], 0.0, S1[:, 0:W], OP.max, OP.add),
                reads=[B("pd"), B("s1")], writes=[pgb])
            R.emit("sync", lambda s, cc=co, ww=w, p_=pg: s.dma_start(
                out=h_phiq[4 * ww:4 * ww + 4, :, cc * P:(cc + 1) * P].rearrange("i p q -> p i q"),
                in_=p_.rearrange("p (i q) -> p i q", q=P)),
                reads=[pgb], writes=[B("phiq_h")], dma=True)

        for i in range(4):
            t = 4 * w + i
            def kvw_mm(t_, ii, wbuf, psum, brofs):
                ins = None
                for j in range(2):
                    sl = slice(j * 512, (j + 1) * 512)
                    for c in range(NCH):
                        ins = t_.matmul(psum[:, sl], wins[:, c * W + ii * P:c * W + ii * P + P],
                                        wbuf[:, c * D:(c + 1) * D][:, sl], start=(c == 0), stop=False)
                    ins = t_.matmul(psum[:, sl], brow[0:1, B_ones:B_ones + P],
                                    brow[0:1, brofs:brofs + D][:, sl], start=False, stop=True)
                return ins
            R.emit("tensor", lambda t_, ii=i: kvw_mm(t_, ii, wts_b, ps_a, B_wk),
                   reads=[B("wins"), B("wins1"), B("wb"), B("br_k"), B("br_ones")], writes=[B("pa")])
            R.emit("vector", lambda v: v.tensor_scalar(S0, ps_a[:], 0.0, None, OP.min),
                   reads=[B("pa")], writes=[B("s0")])
            R.emit("scalar", lambda s: s.activation(S1, S0, AF.Exp), reads=[B("s0")], writes=[B("s1")])
            R.emit("vector", lambda v: v.scalar_tensor_tensor(M0, ps_a[:], 0.0, S1, OP.max, OP.add),
                   reads=[B("pa"), B("s1")], writes=[B("m0")])
            R.emit("tensor", lambda t_, ii=i: kvw_mm(t_, ii, wts_c, ps_b, B_wv),
                   reads=[B("wins"), B("wins1"), B("wc"), B("br_v"), B("br_ones")], writes=[B("pb")])
            def v65f(v):
                src = ps_b[:].rearrange("p (h e) -> p h e", e=DH)
                dst = V65.rearrange("p (h e) -> p h e", e=65)[:, :, 0:DH]
                return v.tensor_copy(dst, src)
            R.emit("vector", v65f, reads=[B("pb")], writes=[B("v65")])
            def kvmm(t_):
                ins = None
                for h in range(H):
                    half, j = h % 2, h // 2
                    base = 64 * half
                    ins = t_.matmul(ps_c[base:base + 64, 64 * j:64 * j + 64],
                                    M0[:, DH * h:DH * h + DH], V65[:, 65 * h:65 * h + DH],
                                    start=True, stop=True, tile_position=(0, base))
                    ins = t_.matmul(ps_c[base:base + 64, 512 + j:513 + j],
                                    M0[:, DH * h:DH * h + DH], V65[:, 64:65],
                                    start=True, stop=True, tile_position=(0, base))
                return ins
            R.emit("tensor", kvmm, reads=[B("m0"), B("v65")], writes=[B("pc")])
            R.emit("vector", lambda v: v.tensor_tensor(small[:, C_kv:C_kv + 520], small[:, C_kv:C_kv + 520],
                                                       ps_c[:, 0:520], OP.add),
                   reads=[B("pc"), B("kv")], writes=[B("kv")])

    R.emit("gpsimd", lambda g: g.dma_start(out=wts_c[:], in_=h_wo.rearrange("(c p) n -> p c n", p=P)),
           writes=[B("wc")], dma=True)
    R.emit("vector", lambda v: v.memset(wts_b[:], 0.0), writes=[B("wb")])
    def wbd(v):
        ins = None
        for c in range(NCH):
            base = c * D + 128 * c
            ins = v.tensor_copy(wts_b[0:64, base:base + DH], small[0:64, C_kv + 64 * c:C_kv + 64 * c + DH])
            ins = v.tensor_copy(wts_b[64:128, base + DH:base + 2 * DH],
                                small[64:128, C_kv + 64 * c:C_kv + 64 * c + DH])
        return ins
    R.emit("vector", wbd, reads=[B("kv")], writes=[B("wb")])
    R.emit("vector", lambda v: v.memset(wts_a[:, 0:128], 0.0), writes=[B("wa")])
    def wz(v):
        ins = None
        for c in range(NCH):
            ins = v.tensor_copy(wts_a[0:64, 16 * c + 2 * c:16 * c + 2 * c + 1],
                                small[0:64, C_kv + 512 + c:C_kv + 512 + c + 1])
            ins = v.tensor_copy(wts_a[64:128, 16 * c + 2 * c + 1:16 * c + 2 * c + 2],
                                small[64:128, C_kv + 512 + c:C_kv + 512 + c + 1])
        return ins
    R.emit("vector", wz, reads=[B("kv")], writes=[B("wa")])
    if debug:
        R.emit("sync", lambda s: s.dma_start(out=h_dbg_wbd[:], in_=wts_b[:]), reads=[B("wb")], dma=True)
        R.emit("sync", lambda s: s.dma_start(out=h_dbg_wz[:], in_=wts_a[:, 0:128]), reads=[B("wa")], dma=True)

    def fin_p2(t, MP, MPB):
        i = t % 4
        dst = wins[:].rearrange("p (c q) -> p c q", q=W)[:, :, i * P:(i + 1) * P]
        transpose8(MP, [MPB], dst, [B("wins"), B("wins1")], evac="scalar")
        def womm(t_, ii=i):
            ins = None
            for j in range(2):
                sl = slice(j * 512, (j + 1) * 512)
                for c in range(NCH):
                    ins = t_.matmul(ps_b[:, sl], wins[:, c * W + ii * P:c * W + ii * P + P],
                                    wts_c[:, c * D:(c + 1) * D][:, sl], start=(c == 0), stop=(c == NCH - 1))
            return ins
        R.emit("tensor", womm, reads=[B("wins"), B("wins1"), B("wc")], writes=[B("pb")])
        R.emit("vector", lambda v, tt=t: v.tensor_tensor(ttile(ubf, tt), ttile(ubf, tt), ps_b[:], OP.add),
               reads=[B("pb"), UB(t)], writes=[UB(t)])
        R.emit("sync", lambda s, tt=t: s.dma_start(out=h_xatt[tt * P:(tt + 1) * P, :], in_=ttile(ubf, tt)),
               reads=[UB(t)], writes=[B("xatt_h")], dma=True)
        R.emit("vector", lambda v, tt=t: v.reduce_sum(stc("sx3", tt), ttile(ubf, tt), axis=AX.X),
               reads=[UB(t)], writes=[B("stats")])
        R.emit("scalar", lambda s, tt=t: s.activation(S0, ttile(ubf, tt), AF.Square, accum_out=stc("sq3", tt)),
               reads=[UB(t)], writes=[B("s0"), B("stats")])

    pend2 = None
    for t in range(NT):
        ft, ftb = FT[t % 2], FTB[t % 2]
        MP, MPB = (M0, B("m0")) if t % 2 == 0 else (M1, B("m1"))
        R.emit("sync", lambda s, tt=t, d=ft: s.dma_start(out=d, in_=h_phiq[tt]),
               reads=[B("phiq_h")], writes=[ftb], dma=True)
        def nmm(t_, d=ft):
            ins = None
            for j in range(2):
                sl = slice(j * 512, (j + 1) * 512)
                for c in range(NCH):
                    ins = t_.matmul(ps_a[:, sl], d[:, c * P:(c + 1) * P],
                                    wts_b[:, c * D:(c + 1) * D][:, sl], start=(c == 0), stop=(c == NCH - 1))
            for c in range(NCH):
                ins = t_.matmul(ps_d[:, 0:16], d[:, c * P:(c + 1) * P],
                                wts_a[:, 16 * c:16 * c + 16], start=(c == 0), stop=(c == NCH - 1))
            return ins
        R.emit("tensor", nmm, reads=[ftb, B("wb"), B("wa")], writes=[B("pa"), B("pd")])
        def divf(v, MM=MP):
            v.tensor_scalar(small[:, C_den:C_den + 16], ps_d[:, 0:16], 1e-6, None, OP.add)
            v.drain()
            v.reciprocal(small[:, C_den + 16:C_den + 32], small[:, C_den:C_den + 16])
            v.drain()
            rb = small[:, C_den + 16:C_den + 32].unsqueeze(2).to_broadcast([P, 16, DH])
            return v.tensor_tensor(MM.rearrange("p (h e) -> p h e", e=DH),
                                   ps_a[:].rearrange("p (h e) -> p h e", e=DH), rb, OP.mult)
        R.emit("vector", divf, reads=[B("pa"), B("pd")], writes=[MPB])
        if pend2 is not None:
            fin_p2(*pend2)
        pend2 = (t, MP, MPB)
    fin_p2(*pend2)
    chain_ln("sx3", "sq3", "mu3", "r3")

    for t in range(NT):
        w, i = t // 4, t % 4
        R.emit("vector", lambda v, tt=t: v.tensor_scalar(
            M0, ttile(ubf, tt), stc("mu3", tt), stc("r3", tt), OP.subtract, OP.mult),
            reads=[UB(t), B("stats")], writes=[B("m0")])
        ft, ftb = FT[t % 2], FTB[t % 2]
        transpose8(M0, [B("m0")], ft.rearrange("p (c q) -> p c q", q=P), [ftb])
        R.emit("sync", lambda s, ww=w, ii=i, d=ft: s.dma_start(
            out=h_h2t[ww].rearrange("p (c q) -> p c q", q=W)[:, :, ii * P:(ii + 1) * P],
            in_=d.rearrange("p (c q) -> p c q", q=P)),
            reads=[ftb], writes=[B("h2t_h")], dma=True)
    R.emit("vector", lambda v: v.memset(ubf[:], 0.0), writes=[UB(t) for t in range(NT)])

    ladder_s = [float(128 >> k) for k in range(8)]
    for Q in range(4):
        R.emit("gpsimd", lambda g, QQ=Q: g.dma_start(
            out=wts_a[:], in_=h_w1[:, QQ * 1024:(QQ + 1) * 1024].rearrange("(c p) n -> p c n", p=P)),
            writes=[B("wa")], dma=True)
        R.emit("gpsimd", lambda g, QQ=Q: g.dma_start(
            out=wts_b[:], in_=h_w2[QQ * 1024:(QQ + 1) * 1024, :].rearrange("(c p) n -> p c n", p=P)),
            writes=[B("wb")], dma=True)
        for c in range(NCH):
            R.emit("vector", lambda v, cc=c: v.tensor_scalar(
                wts_a[:, cc * D:(cc + 1) * D], wts_a[:, cc * D:(cc + 1) * D],
                small[:, C_g2 + cc:C_g2 + cc + 1], None, OP.mult),
                reads=[B("glob")], writes=[B("wa")])
        for j in range(2):
            def mmb(t_, jj=j):
                ins = None
                for c in range(NCH):
                    ins = t_.matmul(ps_d[0:1, 0:512], bcol[:, NCH + c:NCH + c + 1],
                                    wts_a[:, c * D + jj * 512:c * D + jj * 512 + 512],
                                    start=(c == 0), stop=(c == NCH - 1))
                return ins
            R.emit("tensor", mmb, reads=[B("wa"), B("bcol")], writes=[B("pd")])
            R.emit("vector", lambda v, jj=j: v.tensor_copy(
                brow[0:1, B_b2w1 + jj * 512:B_b2w1 + jj * 512 + 512], ps_d[0:1, 0:512]),
                reads=[B("pd")], writes=[B("br_b2w1")])

        def emit_x2(w):
            for i in range(4):
                t = 4 * w + i
                psum, pbn = (ps_a, "pa") if i % 2 == 0 else (ps_b, "pb")
                def xmm_h(t_, ii=i, ps_=psum, h0=0):
                    ins = None
                    for j in range(2):
                        sl = slice(j * 512, (j + 1) * 512)
                        for fc in range(4):
                            ffc = h0 * 4 + fc
                            ins = t_.matmul(ps_[:, sl], wins[:, ffc * W + ii * P:ffc * W + ii * P + P],
                                            wts_b[:, ffc * D:(ffc + 1) * D][:, sl],
                                            start=(ffc == 0 and j == j), stop=(ffc == NCH - 1))
                    return ins
                R.emit("tensor", lambda t_, ii=i, ps_=psum: xmm_h(t_, ii, ps_, 0),
                       reads=[B("wins"), B("wb")], writes=[B(pbn)])
                R.emit("tensor", lambda t_, ii=i, ps_=psum: xmm_h(t_, ii, ps_, 1),
                       reads=[B("wins1"), B("wb")], writes=[B(pbn)])
                R.emit("vector", lambda v, tt=t, ps_=psum: v.tensor_tensor(
                    ttile(ubf, tt), ttile(ubf, tt), ps_[:], OP.add),
                    reads=[B(pbn), UB(t)], writes=[UB(t)])

        WBUFS = [[B("xa0"), B("xa1")], [B("w1r")]]
        CBUFS = [[B("s0"), B("s1")], [B("c1r")]]
        GBUFS = [[B("m0"), B("m1")], [B("gs1")]]
        AWBUF = [B("wins"), B("wins1")]
        pend = None
        for w in range(NW):
            R.emit("sync", lambda s, ww=w: s.dma_start(out=h2w[:], in_=h_h2t[ww]),
                   reads=[B("h2t_h")], writes=[B("h2w")], dma=True)
            for half in range(2):
                for fc in range(4):
                    ffc = half * 4 + fc
                    def amm(t_, fcc=ffc):
                        ins = None
                        for c in range(NCH):
                            ins = t_.matmul(ps_d[:, 0:512], wts_a[:, c * D + fcc * P:c * D + fcc * P + P],
                                            h2w[:, c * W:(c + 1) * W], start=(c == 0), stop=False)
                        return t_.matmul(ps_d[:, 0:512],
                                         brow[0:1, B_b2w1 + fcc * P:B_b2w1 + fcc * P + P],
                                         brow[0:1, B_ones:B_ones + W], start=False, stop=True)
                    R.emit("tensor", amm, reads=[B("wa"), B("h2w"), B("br_b2w1"), B("br_ones")], writes=[B("pd")])
                    R.emit("scalar", lambda s, hh=half, fcc2=fc: s.activation(
                        WARR_H[hh][:, fcc2 * W:fcc2 * W + W], ps_d[:, 0:512], AF.Identity, scale=10.0),
                        reads=[B("pd")], writes=WBUFS[half])
            if pend is not None:
                emit_x2(pend)
            for half in range(2):
                R.emit("scalar", lambda s, hh=half: s.activation(CARRY_H[hh], WARR_H[hh], AF.Identity),
                       reads=WBUFS[half], writes=CBUFS[half])
            for k in range(8):
                for half in range(2):
                    R.emit("scalar", lambda s, hh=half: s.activation(
                        GS_H[hh], CARRY_H[hh], AF.Sign, bias=small[:, C_half:C_half + 1]),
                        reads=CBUFS[half] + [B("glob")], writes=GBUFS[half])
                    R.emit("vector", lambda v, hh=half, kk=k: v.scalar_tensor_tensor(
                        CARRY_H[hh], GS_H[hh], -ladder_s[kk] / 2.0, CARRY_H[hh], OP.mult, OP.add),
                        reads=GBUFS[half] + CBUFS[half], writes=CBUFS[half])
            for half in range(2):
                R.emit("vector", lambda v, hh=half: v.tensor_tensor(
                    CARRY_H[hh], WARR_H[hh], CARRY_H[hh], OP.subtract),
                    reads=WBUFS[half] + CBUFS[half], writes=CBUFS[half])
                R.emit("scalar", lambda s, hh=half: s.activation(
                    wins[:, hh * 2048:(hh + 1) * 2048], CARRY_H[hh], AF.Gelu_apprx_tanh,
                    bias=small[:, C_gb:C_gb + 1], scale=0.1),
                    reads=CBUFS[half] + [B("glob")], writes=[AWBUF[half]])
            pend = w
        emit_x2(pend)

    for t in range(NT):
        R.emit("sync", lambda s, tt=t: s.dma_start(out=M1, in_=h_xatt[tt * P:(tt + 1) * P, :]),
               reads=[B("xatt_h")], writes=[B("m1")], dma=True)
        R.emit("vector", lambda v, tt=t: v.tensor_tensor(ttile(ubf, tt), ttile(ubf, tt), M1, OP.add),
               reads=[B("m1"), UB(t)], writes=[UB(t)])
        R.emit("vector", lambda v, tt=t: v.reduce_sum(stc("sx4", tt), ttile(ubf, tt), axis=AX.X),
               reads=[UB(t)], writes=[B("stats")])
        R.emit("scalar", lambda s, tt=t: s.activation(S0, ttile(ubf, tt), AF.Square, accum_out=stc("sq4", tt)),
               reads=[UB(t)], writes=[B("s0"), B("stats")])
    chain_ln("sx4", "sq4", "mu4", "r4")
    for t in range(NT):
        xa, xab = (XA[0], B("xa0")) if t % 2 == 0 else (XA[1], B("xa1"))
        R.emit("sync", lambda s, tt=t, d=xa: s.dma_start(out=d, in_=h_x[tt * P:(tt + 1) * P, :]),
               writes=[xab], dma=True)
        def xnf(v, tt=t, d=xa):
            v.tensor_scalar(S0, ttile(ubf, tt), stc("mu4", tt), stc("r4", tt), OP.subtract, OP.mult)
            v.drain()
            return v.tensor_tensor(ttile(ubf, tt), S0, d, OP.add)
        R.emit("vector", xnf, reads=[xab, UB(t), B("stats")], writes=[UB(t), B("s0")])
        R.emit("sync", lambda s, tt=t: s.dma_start(out=h_xnew[tt * P:(tt + 1) * P, :], in_=ttile(ubf, tt)),
               reads=[UB(t)], writes=[B("xnew_h")], dma=True)

    for t in range(NT):
        R.emit("vector", lambda v, tt=t: v.reduce_sum(stc("sx5", tt), ttile(ubf, tt), axis=AX.X),
               reads=[UB(t)], writes=[B("stats")])
        R.emit("scalar", lambda s, tt=t: s.activation(S0, ttile(ubf, tt), AF.Square, accum_out=stc("sq5", tt)),
               reads=[UB(t)], writes=[B("s0"), B("stats")])
    chain_ln("sx5", "sq5", "mu5", "r5")
    for t in range(NT):
        R.emit("vector", lambda v, tt=t: v.tensor_scalar(
            M0, ttile(ubf, tt), stc("mu5", tt), stc("r5", tt), OP.subtract, OP.mult),
            reads=[UB(t), B("stats")], writes=[B("m0")])
        ft, ftb = FT[t % 2], FTB[t % 2]
        transpose8(M0, [B("m0")], ft.rearrange("p (c q) -> p c q", q=P), [ftb])
        R.emit("sync", lambda s, tt=t, d=ft: s.dma_start(out=h_u2t[tt], in_=d),
               reads=[ftb], writes=[B("u2t_h")], dma=True)

    R.emit("gpsimd", lambda g: g.dma_start(out=wts_a[:, 0:NE * NCH],
                                           in_=h_gatet.rearrange("(c p) n -> p c n", p=P)),
           writes=[B("wa")], dma=True)
    for t in range(NT):
        ft, ftb = FT[t % 2], FTB[t % 2]
        R.emit("sync", lambda s, tt=t, d=ft: s.dma_start(out=d, in_=h_u2t[tt]),
               reads=[B("u2t_h")], writes=[ftb], dma=True)
        def lmm(t_, d=ft):
            ins = None
            for c in range(NCH):
                ins = t_.matmul(ps_d[:, 0:NE], d[:, c * P:(c + 1) * P], wts_a[:, NE * c:NE * c + NE],
                                start=(c == 0), stop=(c == NCH - 1))
            return ins
        R.emit("tensor", lmm, reads=[ftb, B("wa")], writes=[B("pd")])
        R.emit("vector", lambda v, tt=t: v.tensor_copy(small[:, C_logit + NE * tt:C_logit + NE * tt + NE],
                                                       ps_d[:, 0:NE]),
               reads=[B("pd")], writes=[B("rt")])
    LG = small[:, C_logit:C_logit + 256]
    SC = small[:, C_wts:C_wts + 256]
    MSK = small[:, C_msk:C_msk + 256]
    RS = small[:, C_rsc:C_rsc + NT]
    r3d = lambda ap: ap.rearrange("p (t e) -> p t e", e=NE)
    rb3 = lambda ap: ap.rearrange("p (t o) -> p t o", o=1).to_broadcast([P, NT, NE])
    rt_emit = lambda fn: R.emit("vector", fn, reads=[B("rt")], writes=[B("rt")])
    rt_emit(lambda v: v.reduce_max(RS.rearrange("p (t o) -> p t o", o=1), r3d(LG), axis=AX.X))
    rt_emit(lambda v: v.tensor_tensor(r3d(SC), r3d(LG), rb3(RS), OP.subtract))
    R.emit("scalar", lambda s: s.activation(SC, SC, AF.Exp), reads=[B("rt")], writes=[B("rt")])
    RS2 = small[:, C_rsc2:C_rsc2 + NT]
    def rsum(v):
        v.reduce_sum(RS.rearrange("p (t o) -> p t o", o=1), r3d(SC), axis=AX.X)
        v.drain()
        v.reciprocal(RS2, RS)
        return v.drain()
    rt_emit(rsum)
    rb32 = RS2.rearrange("p (t o) -> p t o", o=1).to_broadcast([P, NT, NE])
    rt_emit(lambda v: v.tensor_tensor(r3d(SC), r3d(SC), rb32, OP.mult))
    rt_emit(lambda v: v.reduce_max(RS.rearrange("p (t o) -> p t o", o=1), r3d(SC), axis=AX.X))
    rt_emit(lambda v: v.tensor_tensor(r3d(MSK), r3d(SC), rb3(RS), OP.is_ge))
    rt_emit(lambda v: v.tensor_tensor(LG, SC, MSK, OP.mult))
    rt_emit(lambda v: v.tensor_tensor(LG, SC, LG, OP.subtract))
    rt_emit(lambda v: v.reduce_max(RS.rearrange("p (t o) -> p t o", o=1), r3d(LG), axis=AX.X))
    def m2f(v):
        v.tensor_tensor(r3d(LG), r3d(LG), rb3(RS), OP.is_ge)
        v.drain()
        v.tensor_tensor(MSK, MSK, LG, OP.add)
        v.drain()
        return v.tensor_tensor(SC, SC, MSK, OP.mult)
    rt_emit(m2f)

    R.emit("vector", lambda v: v.memset(ubf[:], 0.0), writes=[UB(t) for t in range(NT)])

    for e in range(NE):
        R.emit("gpsimd", lambda g, ee=e: g.dma_start(out=wts_a[:],
                                                     in_=h_mw1t[ee].rearrange("(c p) n -> p c n", p=P)),
               writes=[B("wa")], dma=True)
        R.emit("gpsimd", lambda g, ee=e: g.dma_start(out=wts_b[:],
                                                     in_=h_mw2t[ee].rearrange("(c p) n -> p c n", p=P)),
               writes=[B("wb")], dma=True)
        R.emit("gpsimd", lambda g, ee=e: g.dma_start(out=wts_c[:],
                                                     in_=h_mw3t[ee].rearrange("(c p) n -> p c n", p=P)),
               writes=[B("wc")], dma=True)
        R.emit("gpsimd", lambda g, ee=e: g.dma_start(out=betile[:, 0:D], in_=h_be1[ee]),
               writes=[B("betile")], dma=True)
        R.emit("gpsimd", lambda g, ee=e: g.dma_start(out=betile[:, D:2 * D], in_=h_be3[ee]),
               reads=[B("betile")], writes=[B("betile")], dma=True)
        R.emit("gpsimd", lambda g, ee=e: g.dma_start(out=brow[0:1, B_b2e:B_b2e + D], in_=h_be2r[ee]),
               writes=[B("br_b2e")], dma=True)
        HB = [(M0, B("m0")), (h2w[:, 0:1024], B("h2w"))]
        XB3 = [(M1, B("m1")), (h2w[:, 1024:2048], B("h2w"))]
        SB_ = [(S0, B("s0")), (S1, B("s1"))]
        PRB = [(XA[0], B("xa0")), (XA[1], B("xa1"))]
        WSL = [wins[:, 0:1024], wins[:, 1024:2048]]

        def fin_moe(t, mcol):
            pp = t % 2
            hb, hbb = HB[pp]
            xb, xbb = XB3[pp]
            prb, prbb = PRB[pp]
            def tfh(t_, src=hb):
                ins = None
                for c in range(NCH):
                    ins = t_.transpose(ps_t[:, c * P:(c + 1) * P], src[:, c * P:(c + 1) * P], ident[:])
                return ins
            R.emit("tensor", tfh, reads=[hbb, B("ident")], writes=[B("pt")])
            R.emit("scalar", lambda s, ws=WSL[pp]: s.activation(ws, ps_t[:], AF.Identity),
                   reads=[B("pt")], writes=[B("wins")])
            def x2mm(t_, ws=WSL[pp]):
                ins = None
                for j in range(2):
                    sl = slice(j * 512, (j + 1) * 512)
                    for c in range(NCH):
                        ins = t_.matmul(ps_c[:, sl], ws[:, c * P:(c + 1) * P],
                                        wts_b[:, c * D:(c + 1) * D][:, sl], start=(c == 0), stop=False)
                    ins = t_.matmul(ps_c[:, sl], brow[0:1, B_ones:B_ones + P],
                                    brow[0:1, B_b2e:B_b2e + D][:, sl], start=False, stop=True)
                return ins
            R.emit("tensor", x2mm, reads=[B("wins"), B("wb"), B("br_b2e"), B("br_ones")], writes=[B("pc")])
            R.emit("vector", lambda v, mc=mcol, xb_=xb, pr=prb: v.scalar_tensor_tensor(
                pr, ps_c[:], mc, xb_, OP.mult, OP.mult),
                reads=[B("pc"), xbb, B("rt")], writes=[prbb])
            R.emit("vector", lambda v, tt=t, pr=prb: v.tensor_tensor(ttile(ubf, tt), ttile(ubf, tt), pr, OP.add),
                   reads=[prbb, UB(t)], writes=[UB(t)])

        pend4 = None
        for t in range(NT):
            pp = t % 2
            ft, ftb = FT[pp], FTB[pp]
            R.emit("sync", lambda s, tt=t, d=ft: s.dma_start(out=d, in_=h_u2t[tt]),
                   reads=[B("u2t_h")], writes=[ftb], dma=True)
            wcol = small[:, C_wts + NE * t + e:C_wts + NE * t + e + 1]
            mcol = small[:, C_msk + NE * t + e:C_msk + NE * t + e + 1]
            def pmm(t_, d, wbuf, ps_):
                ins = None
                for j in range(2):
                    sl = slice(j * 512, (j + 1) * 512)
                    for c in range(NCH):
                        ins = t_.matmul(ps_[:, sl], d[:, c * P:(c + 1) * P],
                                        wbuf[:, c * D:(c + 1) * D][:, sl],
                                        start=(c == 0), stop=(c == NCH - 1))
                return ins
            hb, hbb = HB[pp]
            xb, xbb = XB3[pp]
            sb_, sbb = SB_[pp]
            R.emit("tensor", lambda t_, d=ft: pmm(t_, d, wts_a, ps_a),
                   reads=[ftb, B("wa")], writes=[B("pa")])
            R.emit("tensor", lambda t_, d=ft: pmm(t_, d, wts_c, ps_b),
                   reads=[ftb, B("wc")], writes=[B("pb")])
            R.emit("vector", lambda v, wc=wcol, s_=sb_: v.scalar_tensor_tensor(
                s_, ps_a[:], wc, betile[:, 0:D], OP.mult, OP.add),
                reads=[B("pa"), B("betile"), B("rt")], writes=[sbb])
            R.emit("scalar", lambda s, s_=sb_, h_=hb: s.activation(h_, s_, AF.Silu),
                   reads=[sbb], writes=[hbb])
            R.emit("vector", lambda v, wc=wcol, x_=xb: v.scalar_tensor_tensor(
                x_, ps_b[:], wc, betile[:, D:2 * D], OP.mult, OP.add),
                reads=[B("pb"), B("betile"), B("rt")], writes=[xbb])
            if pend4 is not None:
                fin_moe(*pend4)
            pend4 = (t, mcol)
        fin_moe(*pend4)
        pend4 = None

    for hf in range(2):
        R.emit("gpsimd", lambda g, h_=hf: g.dma_start(
            out=wts_a[:], in_=h_sw1t[:, h_ * 1024:(h_ + 1) * 1024].rearrange("(c p) n -> p c n", p=P)),
            writes=[B("wa")], dma=True)
        R.emit("gpsimd", lambda g, h_=hf: g.dma_start(
            out=wts_b[:], in_=h_sw2t[:, h_ * 1024:(h_ + 1) * 1024].rearrange("(c p) n -> p c n", p=P)),
            writes=[B("wb")], dma=True)
        R.emit("gpsimd", lambda g, h_=hf: g.dma_start(
            out=wts_c[:], in_=h_sw3t[h_ * 1024:(h_ + 1) * 1024, :].rearrange("(c p) n -> p c n", p=P)),
            writes=[B("wc")], dma=True)
        for t in range(NT):
            ft, ftb = FT[t % 2], FTB[t % 2]
            R.emit("sync", lambda s, tt=t, d=ft: s.dma_start(out=d, in_=h_u2t[tt]),
                   reads=[B("u2t_h")], writes=[ftb], dma=True)
            def gmm(t_, d, wbuf, ps_):
                ins = None
                for j in range(2):
                    sl = slice(j * 512, (j + 1) * 512)
                    for c in range(NCH):
                        ins = t_.matmul(ps_[:, sl], d[:, c * P:(c + 1) * P],
                                        wbuf[:, c * D:(c + 1) * D][:, sl],
                                        start=(c == 0), stop=(c == NCH - 1))
                return ins
            R.emit("tensor", lambda t_, d=ft: gmm(t_, d, wts_a, ps_a),
                   reads=[ftb, B("wa")], writes=[B("pa")])
            R.emit("tensor", lambda t_, d=ft: gmm(t_, d, wts_b, ps_b),
                   reads=[ftb, B("wb")], writes=[B("pb")])
            R.emit("scalar", lambda s: s.activation(S0, ps_a[:], AF.Silu), reads=[B("pa")], writes=[B("s0")])
            R.emit("vector", lambda v: v.tensor_tensor(M0, S0, ps_b[:], OP.mult),
                   reads=[B("s0"), B("pb")], writes=[B("m0")])
            transpose8(M0, [B("m0")], wins[:, 0:1024].rearrange("p (c q) -> p c q", q=P), [B("wins")], evac="scalar")
            def zmm(t_):
                ins = None
                for j in range(2):
                    sl = slice(j * 512, (j + 1) * 512)
                    for c in range(NCH):
                        ins = t_.matmul(ps_c[:, sl], wins[:, c * P:(c + 1) * P],
                                        wts_c[:, c * D:(c + 1) * D][:, sl], start=(c == 0), stop=(c == NCH - 1))
                return ins
            R.emit("tensor", zmm, reads=[B("wins"), B("wc")], writes=[B("pc")])
            R.emit("vector", lambda v, tt=t: v.tensor_tensor(ttile(ubf, tt), ttile(ubf, tt), ps_c[:], OP.add),
                   reads=[B("pc"), UB(t)], writes=[UB(t)])

    for t in range(NT):
        R.emit("vector", lambda v, tt=t: v.reduce_sum(stc("sx6", tt), ttile(ubf, tt), axis=AX.X),
               reads=[UB(t)], writes=[B("stats")])
        R.emit("scalar", lambda s, tt=t: s.activation(S0, ttile(ubf, tt), AF.Square, accum_out=stc("sq6", tt)),
               reads=[UB(t)], writes=[B("s0"), B("stats")])
    chain_ln("sx6", "sq6", "mu6", "r6")
    for t in range(NT):
        xa, xab = (XA[0], B("xa0")) if t % 2 == 0 else (XA[1], B("xa1"))
        mb, mbb = (M0, B("m0")) if t % 2 == 0 else (M1, B("m1"))
        R.emit("sync", lambda s, tt=t, mm=mb: s.dma_start(out=mm, in_=h_xnew[tt * P:(tt + 1) * P, :]),
               reads=[B("xnew_h")], writes=[mbb], dma=True)
        def fin(v, tt=t, d=xa, mm=mb):
            v.tensor_scalar(S0, ttile(ubf, tt), stc("mu6", tt), stc("r6", tt), OP.subtract, OP.mult)
            v.drain()
            return v.tensor_tensor(d, S0, mm, OP.add)
        R.emit("vector", fin, reads=[mbb, UB(t), B("stats")], writes=[xab, B("s0")])
        R.emit("sync", lambda s, tt=t, d=xa: s.dma_start(out=h_out[tt * P:(tt + 1) * P, :], in_=d),
               reads=[xab], dma=True)
    if debug:
        R.emit("sync", lambda s: s.dma_start(out=h_dbg_small[:], in_=small[:]),
               reads=[B("stats"), B("rt"), B("kv"), B("glob")], dma=True)
    fin_waits = [(s, R.v(s)) for s in DMA_POOLS["sync"] if R.v(s) > 0]
    R.items["sync"].append((lambda s: s.nop(), fin_waits, None, 0))

    if n_instr_out is not None:
        n_instr_out.update({e: len(R.items[e]) for e in ENGINES})

    with es:
        with nc.Block() as block:
            def mk(eng_name):
                def section(eng):
                    for fn, waits, inc, inc_n in R.items[eng_name]:
                        for s, val in waits:
                            eng.wait_ge(sems[s], val)
                        ins = fn(eng)
                        if inc is not None and inc_n > 0:
                            if inc_n == 16:
                                assert ins is not None, f"{eng_name}: dma inc but fn returned None"
                                ins.then_inc(sems[inc], inc_n)
                            else:
                                eng.drain().then_inc(sems[inc], inc_n)
                return section
            block.sync(mk("sync"))
            block.gpsimd(mk("gpsimd"))
            block.vector(mk("vector"))
            block.scalar(mk("scalar"))
            block.tensor(mk("tensor"))
    return nc


def prep_inputs(inp):
    f = np.float32
    asf = lambda a: np.asarray(a, f)
    x = asf(inp["L0_attn_x"])
    col = lambda v: np.ascontiguousarray(asf(v).reshape(NCH, P).T)
    mb1, mb2, mb3 = asf(inp["L0_ffn_b1"]), asf(inp["L0_ffn_b2"]), asf(inp["L0_ffn_b3"])
    shared = {
        "wq": np.ascontiguousarray(asf(inp["L0_attn_wq"])),
        "wk": np.ascontiguousarray(asf(inp["L0_attn_wk"])),
        "wv": np.ascontiguousarray(asf(inp["L0_attn_wv"])),
        "wo": np.ascontiguousarray(asf(inp["L0_attn_wo"])),
        "g1c": col(inp["L0_attn_ln1_g"]), "b1c": col(inp["L0_attn_ln1_b"]),
        "g2c": col(inp["L0_attn_ln2_g"]), "b2c": col(inp["L0_attn_ln2_b"]),
        "w1": np.ascontiguousarray(asf(inp["L0_attn_w1"])),
        "w2": np.ascontiguousarray(asf(inp["L0_attn_w2"])),
        "gatet": np.ascontiguousarray(asf(inp["L0_ffn_gate_w"]).T),
        "mw1t": np.ascontiguousarray(asf(inp["L0_ffn_w1"]).transpose(0, 2, 1)),
        "mw2t": np.ascontiguousarray(asf(inp["L0_ffn_w2"]).transpose(0, 2, 1)),
        "mw3t": np.ascontiguousarray(asf(inp["L0_ffn_w3"]).transpose(0, 2, 1)),
        "be1": np.ascontiguousarray(np.broadcast_to(mb1[:, None, :], (NE, P, D))),
        "be3": np.ascontiguousarray(np.broadcast_to(mb3[:, None, :], (NE, P, D))),
        "be2r": np.ascontiguousarray(mb2[:, None, :]),
        "sw1t": np.ascontiguousarray(asf(inp["L0_ffn_sw1"]).T),
        "sw2t": np.ascontiguousarray(asf(inp["L0_ffn_sw2"]).T),
        "sw3t": np.ascontiguousarray(asf(inp["L0_ffn_sw3"]).T),
        "ident": np.eye(P, dtype=f),
        "onesrow": np.ones((1, W), dtype=f),
    }
    return [dict(shared, x=np.ascontiguousarray(x[c])) for c in range(x.shape[0])]


_NC_CACHE = [None]


def kernel(**inputs):
    in_maps = prep_inputs(inputs)
    if _NC_CACHE[0] is None:
        _NC_CACHE[0] = build()
    res = run_bass_kernel_spmd(_NC_CACHE[0], in_maps, core_ids=list(range(len(in_maps))))
    return np.stack([np.asarray(r["out"], np.float32) for r in res.results], axis=0)
